# version 23
# baseline (speedup 1.0000x reference)
import numpy as np
from contextlib import ExitStack

import concourse.bass as bass
import concourse.mybir as mybir
from concourse.bass_utils import run_bass_kernel_spmd
from concourse.bass import IndirectOffsetOnAxis

F32 = mybir.dt.float32
AF = mybir.ActivationFunctionType
ALU = mybir.AluOpType
AX = mybir.AxisListType

NCORES = 8
D = 1024
DC = D // 128
QKV = 3 * D
HD = 64
TTOT = 16384

ATTN_TILE = 256
HALO_TILE = 256
LN_TILE = 512
FFN_TILE = 512
E_FFN = 8
E_TOT = 12
DFF = 2048
NQ = 4
QF = DFF // NQ

NEG_BIG = -1.0e30
NSWEEP = 8


class Sem:
    def __init__(self, stack, nc, name):
        self.h = stack.enter_context(nc.semaphore(name))
        self.n = 0

    def inc(self, inst, k=1):
        inst.then_inc(self.h, k)
        self.n += k
        return (self, self.n)


def build_program(TC, HALO):
    nc = bass.Bass("TRN2", target_bir_lowering=False)

    xm_d = nc.dram_tensor("xm", [D, TC], F32, kind="ExternalInput")
    xh_d = nc.dram_tensor("xh", [D, HALO], F32, kind="ExternalInput")
    wqkv_d = nc.dram_tensor("wqkv", [D, QKV], F32, kind="ExternalInput")
    wout_d = nc.dram_tensor("wout", [2 * D, D], F32, kind="ExternalInput")
    w1t_d = nc.dram_tensor("w1t", [E_FFN * D, DFF], F32, kind="ExternalInput")
    w2t_d = nc.dram_tensor("w2t", [E_FFN * DFF, D], F32, kind="ExternalInput")
    gwt_d = nc.dram_tensor("gwt", [D, E_TOT], F32, kind="ExternalInput")
    ebias_d = nc.dram_tensor("ebias", [128, E_TOT], F32, kind="ExternalInput")
    alpha_d = nc.dram_tensor("alphae", [128, DC], F32, kind="ExternalInput")
    beta_d = nc.dram_tensor("betae", [128, DC], F32, kind="ExternalInput")
    hmask_d = nc.dram_tensor("hmask", [128, 1], F32, kind="ExternalInput")
    eye_d = nc.dram_tensor("eye", [128, 128], F32, kind="ExternalInput")
    onesrow_d = nc.dram_tensor("onesrow", [1, 128], F32, kind="ExternalInput")
    onescol_d = nc.dram_tensor("onescol", [128, 1], F32, kind="ExternalInput")
    bsel8_d = nc.dram_tensor("bsel8", [4, 2 * 128], F32, kind="ExternalInput")
    dsel_d = nc.dram_tensor("dsel", [128, 8], F32, kind="ExternalInput")
    lti_d = nc.dram_tensor("lti", [128, 128], mybir.dt.float16, kind="ExternalInput")
    tokin_d = nc.dram_tensor("tokin", [128, 2 * (TC // 128)], mybir.dt.float16, kind="ExternalInput")
    iotac_d = nc.dram_tensor("iotac", [128, 512], mybir.dt.float16, kind="ExternalInput")
    out_d = nc.dram_tensor("out", [TC, D], F32, kind="ExternalOutput")

    stack = ExitStack()

    def mksb(st):
        def sb(name, p, f):
            return st.enter_context(nc.sbuf_tensor(name, [p, f], F32))
        return sb

    sb = mksb(stack)

    def psum(name, p, f):
        return stack.enter_context(nc.psum_tensor(name, [p, f], F32))

    with stack:
        dvf = sb("dvf", 128, DC)
        dvs = sb("dvs", 128, DC)
        hmask = sb("hmaskt", 128, 1)
        eye = sb("eyet", 128, 128)
        onesrow = sb("onesrowt", 1, 128)
        onescol = sb("onescolt", 128, 1)
        bsel8 = sb("bsel8t", 4, 2 * 128)
        dsel = sb("dselt", 128, 8)
        lti = stack.enter_context(nc.sbuf_tensor("ltit", [128, 128], mybir.dt.float16))
        tokin = stack.enter_context(nc.sbuf_tensor("tokint", [128, 2 * (TC // 128)], mybir.dt.float16))
        iotac = stack.enter_context(nc.sbuf_tensor("iotact", [128, 512], mybir.dt.float16))
        ebias = sb("ebiast", 128, E_TOT)
        gwt = sb("gwtt", 128, DC * E_TOT)
        st4 = sb("st4", 128, 4 * DC)
        idxi32 = stack.enter_context(nc.sbuf_tensor("idxi32", [128, 32], mybir.dt.int32))
        idxs32 = stack.enter_context(nc.sbuf_tensor("idxs32", [128, 32], mybir.dt.int32))
        idxtmp = sb("idxtmp", 128, 1)
        vmf = sb("vmf", 128, 32)

        mst = ExitStack()
        msb = mksb(mst)

        pq0 = psum("pq0", 128, 512)
        pq1 = psum("pq1", 128, 512)
        pbc2 = psum("pbc2", 128, 1024)
        pat = psum("pat", 128, 4 * 512)
        pq = [pq0, pq1]

        s_ld = Sem(stack, nc, "s_ld")
        s_st = Sem(stack, nc, "s_st")

        def dbcast(vec, col, n):
            return bass.AP(vec, col, [[vec.shape[1], 128], [0, n]])

        s0_ac = Sem(stack, nc, "s0_ac")
        with nc.Block() as b:
            @b.sync
            def _(e):
                for dst, src in [(eye, eye_d), (onesrow, onesrow_d), (onescol, onescol_d),
                                 (bsel8, bsel8_d), (dsel, dsel_d), (lti, lti_d),
                                 (tokin, tokin_d), (iotac, iotac_d),
                                 (ebias, ebias_d), (hmask, hmask_d),
                                 (dvf, alpha_d), (dvs, beta_d)]:
                    s_ld.inc(e.dma_start(dst[:], src[:]), 16)
                s_ld.inc(e.dma_start(
                    bass.AP(gwt, 0, [[DC * E_TOT, 128], [E_TOT, DC], [1, E_TOT]]),
                    bass.AP(gwt_d, 0, [[E_TOT, 128], [128 * E_TOT, DC], [1, E_TOT]])), 16)
            LD0 = s_ld.n

            @b.scalar
            def _(s):
                s.wait_ge(s_ld.h, LD0)
                s.activation(dvf[:], dvf[:], AF.Sigmoid)
                s0_ac.inc(s.activation(dvs[:], dvs[:], AF.Sigmoid))

            @b.vector
            def _(v):
                v.wait_ge(s0_ac.h, 1)
                v.tensor_scalar_max(dvf[:], dvf[:], 1e-8)
                v.drain()
                v.tensor_scalar_max(dvs[:], dvs[:], 1e-8)
                v.drain()

        acc = msb("acc", 128, DC * TC)

        AT = ATTN_TILE
        NAT = TC // AT
        NHH = HALO // AT
        NT = NHH + NAT
        assert NHH % 2 == 0 and NT % 2 == 0
        a_qkv = Sem(stack, nc, "a_qkv")
        a_den = Sem(stack, nc, "a_den")
        a_bc = Sem(stack, nc, "a_bc")
        a_out = Sem(stack, nc, "a_out")
        a_mn = Sem(stack, nc, "a_mn")
        a_phi = Sem(stack, nc, "a_phi")
        a_kv = Sem(stack, nc, "a_kv")
        a_tmp = Sem(stack, nc, "a_tmp")
        a_dden = Sem(stack, nc, "a_dden")
        a_fs = Sem(stack, nc, "a_fs")
        a_acd = Sem(stack, nc, "a_acd")
        a_exp = Sem(stack, nc, "a_exp")
        a_vcp = Sem(stack, nc, "a_vcp")
        a_aca = Sem(stack, nc, "a_aca")

        with ExitStack() as ast:
            asb = mksb(ast)
            BF16 = mybir.dt.bfloat16

            def asbt(name, f):
                return ast.enter_context(nc.sbuf_tensor(name, [128, f], BF16))

            xhi_r = asbt("xhi_r", DC * TC)
            xlo_r = asbt("xlo_r", DC * TC)
            wland = asb("wland", 128, 1024)
            wqh, wql = asbt("wqh", DC * 128), asbt("wql", DC * 128)
            wkh, wkl = asbt("wkh", 2 * DC * 128), asbt("wkl", 2 * DC * 128)
            wvh, wvl = asbt("wvh", 2 * DC * 128), asbt("wvl", 2 * DC * 128)
            woh, wol = asbt("woh", 2 * D), asbt("wol", 2 * D)
            xh_t = asb("xh_t", 128, 2 * 1024)
            xhh = asbt("xhh", 2 * DC * AT)
            xhl = asbt("xhl", 2 * DC * AT)
            qphi = [asb(f"qphi{i}", 128, AT) for i in range(2)]
            runb = [asb(f"runb{i}", 128, 2 * AT) for i in range(2)]
            ksb = [asb(f"ksb{i}", 128, 2 * AT) for i in range(2)]
            tmp1 = asb("tmp1", 128, 2 * AT)
            fsb = asb("fsb", 128, 2 * AT)
            fsh = [asbt(f"fsh{i}", 2 * AT) for i in range(2)]
            fsl = [asbt(f"fsl{i}", 2 * AT) for i in range(2)]
            dengall = asb("dengall", 4, AT)

            tiles = [(sw, i, i < NHH) for sw in range(NSWEEP) for i in range(NT)]
            NG = len(tiles)
            AQa, MNa, EXPa, PHIa, VCPa, TMPa = [], [], [], [], [], []
            DENa, DDENa, BCa, FSa, OUTa, ACCa = [], [], [], [], [], []
            prev_main = []
            cnt = dict(aq=0, mn=0, ex=0, ph=0, vc=0, tp=0, dn=0, dd=0, bc=0,
                       fs=0, ot=0, aca=0, acd=0)
            pm_trk = None
            for g, (sw, i, hal) in enumerate(tiles):
                prev_main.append(pm_trk)
                if hal:
                    cnt["aq"] += 2
                    cnt["mn"] += 1
                    cnt["ex"] += 1
                    cnt["ph"] += 1
                    cnt["vc"] += 1
                    cnt["tp"] += 1
                else:
                    cnt["aq"] += 3
                    cnt["mn"] += 2
                    cnt["ex"] += 2
                    cnt["ph"] += 2
                    cnt["vc"] += 1
                    cnt["tp"] += 1
                    cnt["dn"] += 1
                    cnt["dd"] += 1
                    cnt["bc"] += 2
                    cnt["fs"] += 1
                    cnt["ot"] += 8
                    cnt["aca" if sw == 0 else "acd"] += 1
                    pm_trk = g
                AQa.append(cnt["aq"])
                MNa.append(cnt["mn"])
                EXPa.append(cnt["ex"])
                PHIa.append(cnt["ph"])
                VCPa.append(cnt["vc"])
                TMPa.append(cnt["tp"])
                DENa.append(cnt["dn"])
                DDENa.append(cnt["dd"])
                BCa.append(cnt["bc"])
                FSa.append(cnt["fs"])
                OUTa.append(cnt["ot"])
                ACCa.append(cnt["aca"] if sw == 0 else cnt["acd"])
            ACA_TOTAL = cnt["aca"]

            def accsem(g):
                return a_aca if tiles[g][0] == 0 else a_acd

            s_xi = [Sem(stack, nc, f"s_xi{p}") for p in range(2)]
            f_xis = Sem(stack, nc, "f_xis")
            s_wld = Sem(stack, nc, "s_wld")
            f_wgp = Sem(stack, nc, "f_wgp")
            s_xhd = [Sem(stack, nc, f"s_xhd{p}") for p in range(2)]
            f_xhs = Sem(stack, nc, "f_xhs")

            XCH = min(1024, TC)
            NXC = DC * (TC // XCH)
            with nc.Block() as b:
                @b.sync
                def _(e):
                    for ci in range(NXC):
                        dc = ci // (TC // XCH)
                        off = (ci % (TC // XCH)) * XCH
                        if ci >= 2:
                            e.wait_ge(f_xis.h, ci - 1)
                        s_xi[ci % 2].inc(e.dma_start(
                            bass.AP(xh_t, (ci % 2) * 1024, [[2048, 128], [1, XCH]]),
                            bass.AP(xm_d, dc * 128 * TC + off, [[TC, 128], [1, XCH]])),
                            16)

                @b.vector
                def _(v):
                    for ci in range(NXC):
                        dc = ci // (TC // XCH)
                        off = (ci % (TC // XCH)) * XCH
                        v.wait_ge(s_xi[ci % 2].h, 16 * (ci // 2 + 1))
                        sl = (ci % 2) * 1024
                        v.drain()
                        v.tensor_copy(xhi_r[:, dc * TC + off: dc * TC + off + XCH],
                                      xh_t[:, sl:sl + XCH])
                        v.drain()
                        v.tensor_tensor(xlo_r[:, dc * TC + off: dc * TC + off + XCH],
                                        xh_t[:, sl:sl + XCH],
                                        xhi_r[:, dc * TC + off: dc * TC + off + XCH],
                                        ALU.subtract)
                        f_xis.inc(v.drain())

            WGK = [5 * sw + 2 for sw in range(NSWEEP)]
            WGQ = [5 * sw + 3 for sw in range(NSWEEP)]
            WGO = [5 * sw + 5 for sw in range(NSWEEP)]

            with nc.Block() as b:
                @b.sync
                def _(e):
                    xch = [0]

                    def load_xh_half(sw, i, h):
                        ch = xch[0]
                        xch[0] += 1
                        if ch >= 2:
                            e.wait_ge(f_xhs.h, ch - 1)
                        s_xhd[ch % 2].inc(e.dma_start(
                            bass.AP(xh_t, (ch % 2) * 1024,
                                    [[2048, 128], [256, 4], [1, AT]]),
                            bass.AP(xh_d, 4 * h * 128 * HALO + i * AT,
                                    [[HALO, 128], [128 * HALO, 4], [1, AT]])), 16)

                    wc = [0]
                    for sw in range(NSWEEP):
                        for i in range(min(2, NHH)):
                            load_xh_half(sw, i, 0)
                            load_xh_half(sw, i, 1)
                        for off in (D + 128 * sw, 2 * D + 128 * sw, 128 * sw):
                            if wc[0] >= 1:
                                e.wait_ge(f_wgp.h, wc[0])
                            s_wld.inc(e.dma_start(
                                bass.AP(wland, 0, [[1024, 128], [128, DC], [1, 128]]),
                                bass.AP(wqkv_d, off,
                                        [[QKV, 128], [128 * QKV, DC], [1, 128]])), 16)
                            wc[0] += 1
                        for i in range(2, NHH):
                            load_xh_half(sw, i, 0)
                            load_xh_half(sw, i, 1)
                        for kc in range(2):
                            if wc[0] >= 1:
                                e.wait_ge(f_wgp.h, wc[0])
                            s_wld.inc(e.dma_start(
                                bass.AP(wland, 0, [[1024, 128], [1, 1024]]),
                                bass.AP(wout_d, (kc * D + 128 * sw) * D,
                                        [[D, 128], [1, 1024]])), 16)
                            wc[0] += 1

                @b.gpsimd
                def _(gp):
                    xch = [0]
                    wc = [0]

                    def split_xh(sw, i, h):
                        ch = xch[0]
                        xch[0] += 1
                        hidx = sw * NHH + i
                        g = sw * NT + i
                        par = g % 2
                        gp.wait_ge(s_xhd[ch % 2].h, 16 * (ch // 2 + 1))
                        if hidx >= 2:
                            gpv = (hidx - 2) // NHH * NT + (hidx - 2) % NHH
                            gp.wait_ge(a_qkv.h, AQa[gpv])
                        sl = (ch % 2) * 1024
                        dst = par * DC * AT + h * 1024
                        gp.drain()
                        gp.tensor_copy(xhh[:, dst:dst + 1024], xh_t[:, sl:sl + 1024])
                        gp.drain()
                        gp.tensor_tensor(xhl[:, dst:dst + 1024], xh_t[:, sl:sl + 1024],
                                         xhh[:, dst:dst + 1024], ALU.subtract)
                        f_xhs.inc(gp.drain())

                    def split_w(hi_t, lo_t, off, n, rel, relv):
                        gp.wait_ge(s_wld.h, 16 * (wc[0] + 1))
                        wc[0] += 1
                        if rel is not None:
                            gp.wait_ge(rel.h, relv)
                        gp.drain()
                        gp.tensor_copy(hi_t[:, off:off + n], wland[:, 0:n])
                        gp.drain()
                        gp.tensor_tensor(lo_t[:, off:off + n], wland[:, 0:n],
                                         hi_t[:, off:off + n], ALU.subtract)
                        f_wgp.inc(gp.drain())

                    for sw in range(NSWEEP):
                        po = (sw % 2) * 1024
                        kvrel = AQa[(sw - 1) * NT - 1] if sw >= 2 else None
                        aqrel = AQa[sw * NT - 1] if sw >= 1 else None
                        otrel = OUTa[sw * NT - 1] if sw >= 1 else None
                        for i in range(min(2, NHH)):
                            split_xh(sw, i, 0)
                            split_xh(sw, i, 1)
                        split_w(wkh, wkl, po, 1024,
                                a_qkv if sw >= 2 else None, kvrel)
                        split_w(wvh, wvl, po, 1024, None, None)
                        for i in range(2, NHH):
                            split_xh(sw, i, 0)
                            split_xh(sw, i, 1)
                        split_w(wqh, wql, 0, 1024,
                                a_qkv if sw >= 1 else None, aqrel)
                        for kc in range(2):
                            split_w(woh, wol, kc * D, 1024,
                                    a_out if sw >= 1 and kc == 0 else None, otrel)
                @b.tensor
                def _(te):
                    def den_part(g):
                        sw, i, _ = tiles[g]
                        par = g % 2
                        pm = prev_main[g]
                        if i == NHH:
                            te.wait_ge(f_wgp.h, WGO[sw])
                        te.wait_ge(a_tmp.h, TMPa[g])
                        if pm is not None:
                            te.wait_ge(accsem(pm).h, ACCa[pm])
                        te.matmul(pat[0:4, 0:AT], dsel[:, 0:4],
                                  tmp1[:, 0:AT], start=True, stop=False)
                        a_den.inc(te.matmul(pat[0:4, 0:AT], dsel[:, 4:8],
                                            tmp1[:, AT:2 * AT], start=False,
                                            stop=True))

                    def bc_part(g):
                        par = g % 2
                        pm = prev_main[g]
                        te.wait_ge(a_dden.h, DDENa[g])
                        for gi in range(2):
                            if pm is not None:
                                te.wait_ge(a_fs.h, FSa[pm])
                            a_bc.inc(te.matmul(
                                pbc2[:, gi * 512:gi * 512 + AT],
                                bsel8[:, gi * 128:(gi + 1) * 128],
                                dengall[:, 0:AT], start=True, stop=True))

                    def out_part(g):
                        par = g % 2
                        pm = prev_main[g]
                        te.wait_ge(a_fs.h, FSa[g])
                        if pm is not None:
                            te.wait_ge(accsem(pm).h, ACCa[pm])
                        for m in range(8):
                            nmm = 0
                            for kc in range(2):
                                for A, Bv in ((woh, fsh), (woh, fsl), (wol, fsh)):
                                    inst = te.matmul(
                                        pat[:, m * AT:(m + 1) * AT],
                                        A[:, kc * D + m * 128: kc * D + (m + 1) * 128],
                                        Bv[par][:, kc * AT:(kc + 1) * AT],
                                        start=(nmm == 0), stop=(nmm == 5))
                                    nmm += 1
                            a_out.inc(inst)

                    pend = None
                    for g, (sw, i, hal) in enumerate(tiles):
                        par = g % 2
                        prv = g >= 1 and not tiles[g - 1][2]
                        if i == 0 and pend is not None:
                            out_part(pend)
                            pend = None
                        if hal:
                            hidx = sw * NHH + i
                            if i == 0:
                                te.wait_ge(f_wgp.h, WGK[sw])
                            te.wait_ge(f_xhs.h, 2 * (hidx + 1))
                            po = (sw % 2) * 1024
                            groups = [(1, wkh, wkl, po), (2, wvh, wvl, po)]
                        else:
                            tt = (i - NHH) * AT
                            if i == NHH:
                                te.wait_ge(f_wgp.h, WGQ[sw])
                            po = (sw % 2) * 1024
                            groups = [(0, wqh, wql, 0), (1, wkh, wkl, po),
                                      (2, wvh, wvl, po)]
                        first = True
                        for m, wh_, wl_, wb in groups:
                            if m == 0 and g >= 1:
                                te.wait_ge(a_vcp.h, VCPa[g - 1])
                            if m == 1 and g >= 1:
                                te.wait_ge(a_phi.h, PHIa[g - 1])
                            if m == 2:
                                if hal:
                                    if g >= 1:
                                        te.wait_ge(a_vcp.h, VCPa[g - 1])
                                else:
                                    te.wait_ge(a_phi.h, PHIa[g - 1] + 1)
                            if m == 2 and prv:
                                den_part(g - 1)
                            if hal:
                                xh_ = lambda dc: xhh[:, par * DC * AT + dc * AT:
                                                     par * DC * AT + (dc + 1) * AT]
                                xl_ = lambda dc: xhl[:, par * DC * AT + dc * AT:
                                                     par * DC * AT + (dc + 1) * AT]
                            else:
                                xh_ = lambda dc: xhi_r[:, dc * TC + tt:
                                                       dc * TC + tt + AT]
                                xl_ = lambda dc: xlo_r[:, dc * TC + tt:
                                                       dc * TC + tt + AT]
                            nmm = 0
                            for dc in range(DC):
                                for A, Bv in ((wh_, xh_), (wh_, xl_), (wl_, xh_)):
                                    inst = te.matmul(
                                        pq[[0, 1, 0][m]][:, 0:AT],
                                        A[:, wb + dc * 128: wb + (dc + 1) * 128],
                                        Bv(dc),
                                        start=(nmm == 0), stop=(nmm == 3 * DC - 1))
                                    nmm += 1
                            a_qkv.inc(inst)
                            if first:
                                first = False
                                if pend is not None:
                                    out_part(pend)
                                    pend = None
                        if prv:
                            bc_part(g - 1)
                            pend = g - 1
                    if pend is not None:
                        out_part(pend)
                    den_part(NG - 1)
                    bc_part(NG - 1)
                    out_part(NG - 1)

                @b.vector
                def _(v):
                    def v_tail_den(g):
                        par = g % 2
                        v.wait_ge(a_den.h, DENa[g])
                        v.tensor_scalar_max(dengall[:, 0:AT], pat[0:4, 0:AT],
                                            1e-6)
                        v.drain()
                        v.reciprocal(dengall[:, 0:AT], dengall[:, 0:AT])
                        a_dden.inc(v.drain())

                    def qrep(par):
                        return bass.AP(qphi[par], 0, [[AT, 128], [0, 2], [1, AT]])

                    def v_fs(g):
                        par = g % 2
                        v.wait_ge(a_bc.h, BCa[g])
                        v.drain()
                        v.tensor_tensor(fsb[:, 0:2 * AT], qrep(par),
                                        runb[par][:, 0:2 * AT], ALU.mult)
                        v.drain()
                        v.tensor_tensor(
                            fsb[:, 0:2 * AT], fsb[:, 0:2 * AT],
                            bass.AP(pbc2, 0, [[1024, 128], [512, 2], [1, AT]]),
                            ALU.mult)
                        v.drain()
                        v.tensor_copy(fsh[par][:, 0:2 * AT], fsb[:, 0:2 * AT])
                        v.drain()
                        a_fs.inc(v.tensor_tensor(
                            fsl[par][:, 0:2 * AT], fsb[:, 0:2 * AT],
                            fsh[par][:, 0:2 * AT], ALU.subtract))

                    def v_acc(g):
                        sw, i, _ = tiles[g]
                        tt = (i - NHH) * AT
                        if sw > 0:
                            if sw == 1 and i == NHH:
                                v.wait_ge(a_aca.h, ACA_TOTAL)
                            v.wait_ge(a_out.h, OUTa[g])
                            v.drain()
                            acc_ap = bass.AP(acc, tt,
                                             [[DC * TC, 128], [TC, 8], [1, AT]])
                            a_acd.inc(v.scalar_tensor_tensor(
                                acc_ap,
                                bass.AP(pat, 0, [[4 * 512, 128], [AT, 8], [1, AT]]),
                                0.0, acc_ap, ALU.bypass, ALU.add))

                    pend_acc = [None]
                    for g, (sw, i, hal) in enumerate(tiles):
                        par = g % 2
                        if g >= 2:
                            v.wait_ge(a_out.h, OUTa[g - 2])
                            v.wait_ge(a_bc.h, BCa[g - 2])
                            v.wait_ge(a_den.h, DENa[g - 2])
                            v.wait_ge(a_exp.h, EXPa[g - 2])
                        aqb = AQa[g - 1] if g >= 1 else 0
                        prv = g >= 1 and not tiles[g - 1][2]
                        if hal:
                            v.wait_ge(a_qkv.h, aqb + 1)
                            a_mn.inc(v.tensor_scalar_min(
                                ksb[par][:, AT:2 * AT], pq1[:, 0:AT], 0.0))
                            a_phi.inc(v.tensor_scalar_max(
                                ksb[par][:, 0:AT], pq1[:, 0:AT], 0.0))
                        else:
                            for j in range(2):
                                dst = qphi[par][:, 0:AT] if j == 0 \
                                    else ksb[par][:, AT:2 * AT]
                                sta = runb[par][:, 0:AT] if j == 0 \
                                    else ksb[par][:, 0:AT]
                                v.wait_ge(a_qkv.h, aqb + j + 1)
                                a_mn.inc(v.tensor_scalar_min(
                                    dst, pq[j % 2][:, 0:AT], 0.0))
                                a_phi.inc(v.tensor_scalar_max(
                                    sta, pq[j % 2][:, 0:AT], 0.0))
                        if pend_acc[0] is not None:
                            v_acc(pend_acc[0])
                            pend_acc[0] = None
                        if prv:
                            v_tail_den(g - 1)
                            v_fs(g - 1)
                        if hal:
                            v.wait_ge(a_exp.h, EXPa[g])
                            v.drain()
                            v.tensor_tensor(ksb[par][:, AT:2 * AT],
                                            ksb[par][:, AT:2 * AT],
                                            ksb[par][:, 0:AT], ALU.add)
                            v.drain()
                            v.tensor_scalar_mul(ksb[par][:, AT:2 * AT],
                                                ksb[par][:, AT:2 * AT],
                                                hmask[:, 0:1])
                        else:
                            for j in range(2):
                                dst = qphi[par][:, 0:AT] if j == 0 \
                                    else ksb[par][:, AT:2 * AT]
                                sta = runb[par][:, 0:AT] if j == 0 \
                                    else ksb[par][:, 0:AT]
                                v.wait_ge(a_exp.h, EXPa[g] - 2 + j + 1)
                                v.drain()
                                v.tensor_tensor(dst, dst, sta, ALU.add)
                        v.wait_ge(a_vcp.h, VCPa[g])
                        v.drain()
                        a_kv.inc(v.tensor_tensor(runb[par][:, AT:2 * AT],
                                                 ksb[par][:, AT:2 * AT],
                                                 runb[par][:, AT:2 * AT], ALU.mult))
                        ini = (i == 0)
                        v.drain()
                        v.tensor_tensor_scan(
                            runb[par][:, 0:AT], dbcast(dvf, sw, AT),
                            runb[par][:, AT:2 * AT],
                            0.0 if ini else st4[:, sw:sw + 1],
                            ALU.mult, ALU.add)
                        v.drain()
                        v.tensor_tensor_scan(
                            runb[par][:, AT:2 * AT], dbcast(dvs, sw, AT),
                            runb[par][:, AT:2 * AT],
                            0.0 if ini else st4[:, DC + sw:DC + sw + 1],
                            ALU.mult, ALU.add)
                        v.drain()
                        v.tensor_copy(
                            bass.AP(st4, sw, [[4 * DC, 128], [DC, 2], [1, 1]]),
                            bass.AP(runb[par], AT - 1, [[2 * AT, 128], [AT, 2], [1, 1]]))
                        v.drain()
                        v.tensor_tensor_scan(
                            ksb[par][:, 0:AT], dbcast(dvf, sw, AT),
                            ksb[par][:, AT:2 * AT],
                            0.0 if ini else st4[:, 2 * DC + sw:2 * DC + sw + 1],
                            ALU.mult, ALU.add)
                        v.drain()
                        v.tensor_tensor_scan(
                            ksb[par][:, AT:2 * AT], dbcast(dvs, sw, AT),
                            ksb[par][:, AT:2 * AT],
                            0.0 if ini else st4[:, 3 * DC + sw:3 * DC + sw + 1],
                            ALU.mult, ALU.add)
                        v.drain()
                        v.tensor_copy(
                            bass.AP(st4, 2 * DC + sw, [[4 * DC, 128], [DC, 2], [1, 1]]),
                            bass.AP(ksb[par], AT - 1, [[2 * AT, 128], [AT, 2], [1, 1]]))
                        if not hal:
                            v.drain()
                            v.tensor_tensor(tmp1[:, 0:2 * AT], qrep(par),
                                            ksb[par][:, 0:2 * AT], ALU.mult)
                        a_tmp.inc(v.drain())
                        if prv:
                            pend_acc[0] = g - 1
                    if pend_acc[0] is not None:
                        v_acc(pend_acc[0])
                    v_tail_den(NG - 1)
                    v_fs(NG - 1)
                    v_acc(NG - 1)

                @b.scalar
                def _(s):
                    def acc_copies(g):
                        tp = (tiles[g][1] - NHH) * AT
                        s.wait_ge(a_out.h, OUTa[g])
                        a_aca.inc(s.copy(
                            bass.AP(acc, tp, [[DC * TC, 128], [TC, 8], [1, AT]]),
                            bass.AP(pat, 0, [[4 * 512, 128], [AT, 8], [1, AT]])))

                    for g, (sw, i, hal) in enumerate(tiles):
                        par = g % 2
                        if hal:
                            s.wait_ge(a_mn.h, MNa[g])
                            a_exp.inc(s.activation(ksb[par][:, AT:2 * AT],
                                                   ksb[par][:, AT:2 * AT], AF.Exp))
                        else:
                            for j in range(2):
                                dst = qphi[par][:, 0:AT] if j == 0 \
                                    else ksb[par][:, AT:2 * AT]
                                s.wait_ge(a_mn.h, MNa[g] - 2 + j + 1)
                                a_exp.inc(s.activation(dst, dst, AF.Exp))
                        s.wait_ge(a_qkv.h, AQa[g])
                        if g >= 2:
                            s.wait_ge(a_fs.h, FSa[g - 2])
                            s.wait_ge(a_tmp.h, TMPa[g - 2])
                        a_vcp.inc(s.copy(runb[par][:, AT:2 * AT], pq0[:, 0:AT]))
                        if g >= 1 and not tiles[g - 1][2] and tiles[g - 1][0] == 0:
                            acc_copies(g - 1)
                    if not tiles[NG - 1][2] and tiles[NG - 1][0] == 0:
                        acc_copies(NG - 1)


        x1 = msb("x1", 128, DC * TC)

        def emit_ln(res_src, tag):
            NLT = TC // LN_TILE
            l_add = Sem(stack, nc, f"l_add{tag}")
            l_s1 = Sem(stack, nc, f"l_s1{tag}")
            l_s2 = Sem(stack, nc, f"l_s2{tag}")
            l_sq = Sem(stack, nc, f"l_sq{tag}")
            l_st1 = Sem(stack, nc, f"l_st1{tag}")
            l_sqr = Sem(stack, nc, f"l_sqr{tag}")
            l_st2 = Sem(stack, nc, f"l_st2{tag}")
            l_bc = Sem(stack, nc, f"l_bc{tag}")
            l_nrm = Sem(stack, nc, f"l_nrm{tag}")
            LT = LN_TILE

            with ExitStack() as lst:
                lsb = mksb(lst)
                lnsum = lsb(f"lnsum{tag}", 1, LT)
                lnsq = lsb(f"lnsq{tag}", 1, LT)
                lnm2 = lsb(f"lnm2{tag}", 1, LT)
                lnsc = lsb(f"lnsc{tag}", 128, 2 * LT)
                xln = lsb(f"xln{tag}", 128, 2 * DC * LT)
                s_lnx = [Sem(stack, nc, f"s_lnx{tag}{p}") for p in range(2)]

                with nc.Block() as b:
                    @b.sync
                    def _(e):
                        for t in range(NLT):
                            if t >= 2:
                                e.wait_ge(l_add.h, t - 1)
                            s_lnx[t % 2].inc(e.dma_start(
                                bass.AP(xln, (t % 2) * DC * LT,
                                        [[2 * DC * LT, 128], [LT, DC], [1, LT]]),
                                bass.AP(xm_d, t * LT,
                                        [[TC, 128], [128 * TC, DC], [1, LT]])), 16)

                    @b.vector
                    def _(v):
                        for t in range(NLT):
                            tt = t * LT
                            v.wait_ge(s_lnx[t % 2].h, 16 * (t // 2 + 1))
                            for dc in range(DC):
                                v.drain()
                                v.tensor_tensor(
                                    x1[:, dc * TC + tt: dc * TC + tt + LT],
                                    xln[:, (t % 2) * DC * LT + dc * LT:
                                        (t % 2) * DC * LT + (dc + 1) * LT],
                                    res_src[:, dc * TC + tt: dc * TC + tt + LT], ALU.add)
                            l_add.inc(v.drain())
                            v.wait_ge(l_s1.h, t + 1)
                            v.wait_ge(l_s2.h, 8 * (t + 1))
                            v.tensor_scalar_mul(lnsum[:, 0:LT], pq0[0:1, 0:LT], 1.0 / D)
                            v.tensor_scalar_mul(lnsq[:, 0:LT], pq1[0:1, 0:LT], 1.0 / D)
                            v.drain()
                            v.tensor_tensor(lnm2[:, 0:LT], lnsum[:, 0:LT], lnsum[:, 0:LT],
                                            ALU.mult)
                            v.drain()
                            v.tensor_tensor(lnsq[:, 0:LT], lnsq[:, 0:LT], lnm2[:, 0:LT],
                                            ALU.subtract)
                            v.drain()
                            l_st1.inc(v.tensor_scalar_add(lnsq[:, 0:LT], lnsq[:, 0:LT],
                                                          1e-5))
                            v.wait_ge(l_sqr.h, t + 1)
                            v.reciprocal(lnsq[:, 0:LT], lnsq[:, 0:LT])
                            v.drain()
                            v.tensor_tensor(lnm2[:, 0:LT], lnsum[:, 0:LT], lnsq[:, 0:LT],
                                            ALU.mult)
                            l_st2.inc(v.drain())
                            v.wait_ge(l_bc.h, 2 * (t + 1))
                            for dc in range(DC):
                                v.drain()
                                v.tensor_tensor(
                                    x1[:, dc * TC + tt: dc * TC + tt + LT],
                                    x1[:, dc * TC + tt: dc * TC + tt + LT],
                                    pbc2[:, 0:LT], ALU.mult)
                                v.drain()
                                inst = v.tensor_tensor(
                                    x1[:, dc * TC + tt: dc * TC + tt + LT],
                                    x1[:, dc * TC + tt: dc * TC + tt + LT],
                                    pbc2[:, 512:512 + LT], ALU.subtract)
                            l_nrm.inc(inst)

                    @b.tensor
                    def _(te):
                        for t in range(NLT):
                            tt = t * LT
                            te.wait_ge(l_add.h, t + 1)
                            if t >= 1:
                                te.wait_ge(l_st1.h, t)
                            for dc in range(DC):
                                inst = te.matmul(pq0[0:1, 0:LT], onescol[:],
                                                 x1[:, dc * TC + tt: dc * TC + tt + LT],
                                                 start=(dc == 0), stop=(dc == DC - 1))
                            l_s1.inc(inst)
                            for dc in range(DC):
                                te.wait_ge(l_sq.h, 8 * t + dc + 1)
                                l_s2.inc(te.matmul(
                                    pq1[0:1, 0:LT], onescol[:],
                                    lnsc[:, (dc % 2) * LT:(dc % 2 + 1) * LT],
                                    start=(dc == 0), stop=(dc == DC - 1)))
                            te.wait_ge(l_st2.h, t + 1)
                            if t >= 1:
                                te.wait_ge(l_nrm.h, t)
                            l_bc.inc(te.matmul(pbc2[:, 0:LT], onesrow[:], lnsq[0:1, 0:LT],
                                               start=True, stop=True))
                            l_bc.inc(te.matmul(pbc2[:, 512:512 + LT], onesrow[:],
                                               lnm2[0:1, 0:LT], start=True, stop=True))

                    @b.scalar
                    def _(s):
                        for t in range(NLT):
                            tt = t * LT
                            s.wait_ge(l_add.h, t + 1)
                            for dc in range(DC):
                                if 8 * t + dc >= 2:
                                    s.wait_ge(l_s2.h, 8 * t + dc - 1)
                                l_sq.inc(s.activation(
                                    lnsc[:, (dc % 2) * LT:(dc % 2 + 1) * LT],
                                    x1[:, dc * TC + tt: dc * TC + tt + LT], AF.Square))
                            s.wait_ge(l_st1.h, t + 1)
                            l_sqr.inc(s.activation(lnsq[:, 0:LT], lnsq[:, 0:LT], AF.Sqrt))

        emit_ln(acc, "A")

        NGC = TC // 128
        CAP = 512
        NBLK = CAP // 128
        XR = D + E_TOT
        xext_d = nc.dram_tensor("xext", [TC, XR], F32, kind="Internal")
        acc_d = nc.dram_tensor("accd", [TC + 1, D], F32, kind="Internal")

        g_pe = Sem(stack, nc, "g_pe")
        g_ev = Sem(stack, nc, "g_ev")
        g_dv1 = Sem(stack, nc, "g_dv1")
        g_ac = Sem(stack, nc, "g_ac")
        g_dv2 = Sem(stack, nc, "g_dv2")
        g_rk = Sem(stack, nc, "g_rk")
        g_chb = Sem(stack, nc, "g_chb")
        g_dvp = Sem(stack, nc, "g_dvp")
        g_chf = Sem(stack, nc, "g_chf")

        r_pd = Sem(stack, nc, "r_pd")
        r_pe = Sem(stack, nc, "r_pe")
        r_ev = Sem(stack, nc, "r_ev")
        r_tr = Sem(stack, nc, "r_tr")
        r_ec = Sem(stack, nc, "r_ec")

        with ExitStack() as gst:
            gsb = mksb(gst)
            glog = gsb("glog", 128, E_TOT)
            gm1 = gsb("gm1", 128, 1)
            gnm1 = gsb("gnm1", 128, 1)
            gm2 = gsb("gm2", 128, 1)
            gex = gsb("gex", 128, E_TOT)
            gtmp = gsb("gtmp", 128, E_TOT)
            gden = gsb("gden", 128, 1)
            gwtok = gsb("gwtok", 128, E_TOT)
            gmall = gsb("gmall", 128, NGC * 8)
            gmi16 = gst.enter_context(nc.sbuf_tensor("gmi16", [128, 8], mybir.dt.float16))
            posall = gsb("posall", 128, NGC * 8)
            gwstash = gsb("gwstash", 128, NGC * E_TOT)
            wztm = gsb("wztm", 128, NGC)
            cntrow = gsb("cntrow", 1, 8)
            chofrow = gsb("chofrow", 1, 8)

            with nc.Block() as b:
                @b.tensor
                def _(te):
                    for c in range(NGC):
                        if c >= 2:
                            te.wait_ge(g_ev.h, c - 1)
                        for dc in range(DC):
                            inst = te.matmul(
                                pq[c % 2][0:128, 0:E_TOT],
                                x1[:, dc * TC + c * 128: dc * TC + (c + 1) * 128],
                                gwt[:, dc * E_TOT:(dc + 1) * E_TOT],
                                start=(dc == 0), stop=(dc == DC - 1))
                        g_pe.inc(inst)
                        te.wait_ge(g_dv2.h, c + 1)
                        if c >= 1:
                            te.wait_ge(g_dvp.h, c)
                            te.wait_ge(g_chf.h, c)
                        te.matmul(pbc2[0:128, 0:8], lti[:], gmi16[:],
                                  start=True, stop=True)
                        g_rk.inc(te.matmul(pbc2[0:1, 512:520], tokin[:, 1:2],
                                           gmi16[:], start=True, stop=True))
                        if c >= 1:
                            te.wait_ge(g_chf.h, c)
                            g_chb.inc(te.matmul(pat[:, 0:8], onesrow[:],
                                                chofrow[0:1, 0:8], start=True, stop=True))

                @b.vector
                def _(v):
                    for c in range(NGC):
                        v.wait_ge(g_pe.h, c + 1)
                        if c >= 1:
                            v.wait_ge(g_ac.h, c)
                        v.drain()
                        g_ev.inc(v.tensor_tensor(glog[:], pq[c % 2][0:128, 0:E_TOT],
                                                 ebias[:], ALU.add))
                        v.drain()
                        v.reduce_max(gm1[:], glog[:], axis=AX.X)
                        v.drain()
                        v.tensor_scalar_mul(gnm1[:], gm1[:], -1.0)
                        v.tensor_scalar(gtmp[:], glog[:], gm1[:, 0:1], None, ALU.is_ge)
                        v.drain()
                        v.scalar_tensor_tensor(gtmp[:], gtmp[:], NEG_BIG, glog[:],
                                               ALU.mult, ALU.add)
                        v.drain()
                        g_dv1.inc(v.reduce_max(gm2[:], gtmp[:], axis=AX.X))
                        v.wait_ge(g_ac.h, c + 1)
                        v.tensor_scalar_add(gden[:], gden[:], 1.0)
                        v.drain()
                        v.reciprocal(gden[:], gden[:])
                        v.tensor_scalar(gwtok[:], glog[:], gm2[:, 0:1], None, ALU.is_ge)
                        v.drain()
                        v.tensor_copy(gmall[:, 8 * c:8 * (c + 1)], gwtok[:, 0:8])
                        v.tensor_scalar(gmi16[:], glog[:, 0:8], gm2[:, 0:1], None,
                                        ALU.is_ge)
                        v.drain()
                        v.tensor_tensor(gwtok[:], gwtok[:], gex[:], ALU.mult)
                        v.drain()
                        g_dv2.inc(v.tensor_scalar_mul(gwtok[:], gwtok[:], gden[:, 0:1]))
                        v.drain()
                        v.tensor_copy(gwstash[:, E_TOT * c:E_TOT * (c + 1)], gwtok[:])
                        v.drain()
                        v.reduce_sum(wztm[:, c:c + 1], gwtok[:, E_FFN:E_TOT], axis=AX.X)
                        v.wait_ge(g_rk.h, c + 1)
                        v.drain()
                        v.tensor_copy(posall[:, 8 * c:8 * (c + 1)], pbc2[0:128, 0:8])
                        if c >= 1:
                            v.wait_ge(g_chb.h, c)
                            v.drain()
                            v.tensor_tensor(posall[:, 8 * c:8 * (c + 1)],
                                            posall[:, 8 * c:8 * (c + 1)],
                                            pat[:, 0:8], ALU.add)
                        g_dvp.inc(v.drain())
                        v.tensor_copy(cntrow[:], pbc2[0:1, 512:520])
                        v.drain()
                        if c == 0:
                            v.tensor_copy(chofrow[:], cntrow[:])
                        else:
                            v.tensor_tensor(chofrow[:], chofrow[:], cntrow[:], ALU.add)
                        g_chf.inc(v.drain())

                @b.scalar
                def _(s):
                    for c in range(NGC):
                        s.wait_ge(g_dv1.h, c + 1)
                        s.activation(gex[:], glog[:], AF.Exp, bias=gnm1[:, 0:1])
                        g_ac.inc(s.activation(gden[:], gm2[:], AF.Exp, bias=gnm1[:, 0:1]))

                @b.gpsimd
                def _(gp):
                    pass


            with ExitStack() as rst:
                rsb = mksb(rst)
                Pb = rst.enter_context(nc.sbuf_tensor("Pb", [128, 2 * CAP], mybir.dt.float16))
                LV = [rsb(f"LV{e}", 2, CAP) for e in range(E_FFN)]

                with nc.Block() as b:
                    @b.vector
                    def _(v):
                        for e_ in range(E_FFN):
                            for c in range(NGC):
                                k = e_ * NGC + c
                                if k >= 2:
                                    v.wait_ge(r_pe.h, k - 1)
                                v.drain()
                                r_pd.inc(v.tensor_scalar(
                                    Pb[:, (c % 2) * CAP:(c % 2 + 1) * CAP],
                                    iotac[:], posall[:, 8 * c + e_:8 * c + e_ + 1],
                                    gmall[:, 8 * c + e_:8 * c + e_ + 1],
                                    ALU.is_equal, ALU.mult))
                            v.wait_ge(r_pe.h, NGC * (e_ + 1))
                            v.drain()
                            r_ev.inc(v.tensor_copy(LV[e_][:], pbc2[0:2, 0:CAP]))
                            for b_ in range(NBLK):
                                j = e_ * NBLK + b_
                                v.wait_ge(r_tr.h, j + 1)
                                v.drain()
                                v.tensor_copy(idxi32[:, j:j + 1],
                                              pat[:, (b_ % 2) * 512:(b_ % 2) * 512 + 1])
                                v.tensor_copy(vmf[:, j:j + 1],
                                              pat[:, (b_ % 2) * 512 + 1:(b_ % 2) * 512 + 2])
                                v.drain()
                                v.scalar_tensor_tensor(
                                    idxtmp[:, 0:1], vmf[:, j:j + 1],
                                    float(-TC), idxi32[:, j:j + 1],
                                    ALU.mult, ALU.add)
                                v.drain()
                                v.tensor_scalar_add(idxtmp[:, 0:1], idxtmp[:, 0:1],
                                                    float(TC))
                                v.drain()
                                r_ec.inc(v.tensor_copy(idxs32[:, j:j + 1],
                                                       idxtmp[:, 0:1]))

                    @b.tensor
                    def _(te):
                        for e_ in range(E_FFN):
                            if e_ >= 1:
                                te.wait_ge(r_ev.h, e_)
                            for c in range(NGC):
                                k = e_ * NGC + c
                                te.wait_ge(r_pd.h, k + 1)
                                r_pe.inc(te.matmul(
                                    pbc2[0:2, 0:CAP], tokin[:, 2 * c:2 * c + 2],
                                    Pb[:, (c % 2) * CAP:(c % 2 + 1) * CAP],
                                    start=(c == 0), stop=(c == NGC - 1)))
                            te.wait_ge(r_ev.h, e_ + 1)
                            for b_ in range(NBLK):
                                j = e_ * NBLK + b_
                                if j >= 2:
                                    te.wait_ge(r_ec.h, j - 1)
                                r_tr.inc(te.transpose(
                                    pat[0:128, (b_ % 2) * 512:(b_ % 2) * 512 + 2],
                                    LV[e_][:, b_ * 128:(b_ + 1) * 128], eye[0:2, 0:2]))

            x_tr = Sem(stack, nc, "x_tr")
            x_ev = Sem(stack, nc, "x_ev")
            x_dv = Sem(stack, nc, "x_dv")
            s_xe = [Sem(stack, nc, "s_xe0"), Sem(stack, nc, "s_xe1")]

            with ExitStack() as xst:
                xsb = mksb(xst)
                xtile = [xsb(f"xtile{i}", 128, XR) for i in range(2)]
                acci = [xsb(f"acci{i}", 128, D) for i in range(2)]

                with nc.Block() as b:
                    @b.tensor
                    def _(te):
                        for c in range(NGC):
                            for dc in range(DC):
                                k = c * DC + dc
                                if k >= 2:
                                    te.wait_ge(x_ev.h, k - 1)
                                x_tr.inc(te.transpose(
                                    pbc2[0:128, (k % 2) * 512:(k % 2) * 512 + 128],
                                    x1[:, dc * TC + c * 128: dc * TC + (c + 1) * 128],
                                    eye[:]))

                    @b.scalar
                    def _(s):
                        for c in range(NGC):
                            for dc in range(DC):
                                k = c * DC + dc
                                s.wait_ge(x_tr.h, k + 1)
                                if c >= 2 and dc == 0:
                                    s.wait_ge(s_xe[c % 2].h, 32 * (c // 2))
                                x_ev.inc(s.copy(
                                    xtile[c % 2][:, dc * 128:(dc + 1) * 128],
                                    pbc2[0:128, (k % 2) * 512:(k % 2) * 512 + 128]))

                    @b.vector
                    def _(v):
                        for c in range(NGC):
                            v.wait_ge(x_ev.h, 8 * (c + 1))
                            if c >= 2:
                                v.wait_ge(s_xe[c % 2].h, 32 * (c // 2))
                            v.drain()
                            v.tensor_copy(xtile[c % 2][:, D:D + E_TOT],
                                          gwstash[:, E_TOT * c:E_TOT * (c + 1)])
                            v.drain()
                            x_dv.inc(v.tensor_scalar_mul(
                                acci[c % 2][:], xtile[c % 2][:, 0:D], wztm[:, c:c + 1]))

                    @b.sync
                    def _(e):
                        for c in range(NGC):
                            e.wait_ge(x_dv.h, c + 1)
                            s_xe[c % 2].inc(e.dma_start(
                                bass.AP(xext_d, c * 128 * XR, [[XR, 128], [1, XR]]),
                                xtile[c % 2][:]), 16)
                            s_xe[c % 2].inc(e.dma_start(
                                bass.AP(acc_d, c * 128 * D, [[D, 128], [1, D]]),
                                acci[c % 2][:]), 16)

        mst.close()

        f_gx = Sem(stack, nc, "f_gx")
        f_sct = Sem(stack, nc, "f_sct")
        f_wc = Sem(stack, nc, "f_wc")
        f_tr = Sem(stack, nc, "f_tr")
        f_te = Sem(stack, nc, "f_te")
        f_h = Sem(stack, nc, "f_h")
        f_sil = Sem(stack, nc, "f_sil")
        f_y = Sem(stack, nc, "f_y")
        f_ye = Sem(stack, nc, "f_ye")
        f_ytr = Sem(stack, nc, "f_ytr")
        f_yte = Sem(stack, nc, "f_yte")
        f_ysc = Sem(stack, nc, "f_ysc")
        f_wsp = Sem(stack, nc, "f_wsp")
        f_xsp = Sem(stack, nc, "f_xsp")
        f_hsp = Sem(stack, nc, "f_hsp")
        s_wl = [Sem(stack, nc, f"s_wl{j}") for j in range(3)]
        QW = DC * QF + NQ * D
        HQW = DC * QF
        assert HQW == NQ * D

        with ExitStack() as fst:
            fsb_ = mksb(fst)
            BF16 = mybir.dt.bfloat16

            def fsbt(name, f):
                return fst.enter_context(nc.sbuf_tensor(name, [128, f], BF16))

            wsrc = fsb_("warena", 128, 3 * HQW)
            whi = fsbt("whi", 2 * QW)
            wlo = fsbt("wlo", 2 * QW)
            xg = fsb_("xg", 128, NBLK * XR)
            xgfm = fsb_("xgfm", 128, DC * CAP)
            xgh = fsbt("xgh", DC * CAP)
            xgl = fsbt("xgl", DC * CAP)
            yfm = fsb_("yfm", 128, 8 * CAP)
            hbuf = fsb_("hbuf", 128, NQ * CAP)
            hbh = fsbt("hbh", NQ * CAP)
            hbl = fsbt("hbl", NQ * CAP)
            wvcol = fsb_("wvcol", 128, NBLK)

            def wap(buf, par, off, m):
                return bass.AP(buf, par * QW + off, [[2 * QW, 128], [1, m]])

            with nc.Block() as b:
                @b.sync
                def _(e):
                    e.wait_ge(s_xe[0].h, 32 * (NGC - NGC // 2))
                    e.wait_ge(s_xe[1].h, 32 * (NGC // 2))
                    for e_ in range(E_FFN):
                        for q_ in range(NQ):
                            idx = e_ * NQ + q_
                            for hh in range(2):
                                j = 2 * idx + hh
                                if j >= 3:
                                    e.wait_ge(f_wsp.h, j - 2)
                                sl = (j % 3) * HQW
                                if hh == 0:
                                    s_wl[j % 3].inc(e.dma_start(
                                        bass.AP(wsrc, sl,
                                                [[3 * HQW, 128], [QF, DC], [1, QF]]),
                                        bass.AP(w1t_d, e_ * D * DFF + q_ * QF,
                                                [[DFF, 128], [128 * DFF, DC], [1, QF]])),
                                        16)
                                else:
                                    s_wl[j % 3].inc(e.dma_start(
                                        bass.AP(wsrc, sl,
                                                [[3 * HQW, 128], [D, NQ], [1, D]]),
                                        bass.AP(w2t_d, (e_ * DFF + q_ * QF) * D,
                                                [[D, 128], [128 * D, NQ], [1, D]])),
                                        16)

                @b.gpsimd
                def _(gp):
                    def gathers(e_):
                        for b_ in range(NBLK):
                            if e_ >= 1:
                                gp.wait_ge(f_tr.h, 32 * (e_ - 1) + 8 * (b_ + 1))
                                gp.wait_ge(f_wc.h, 4 * (e_ - 1) + b_ + 1)
                            f_gx.inc(gp.indirect_dma_start(
                                xg[:, b_ * XR:(b_ + 1) * XR], None,
                                bass.AP(xext_d, 0, [[XR, TC], [1, XR]]),
                                IndirectOffsetOnAxis(
                                    idxi32[:, e_ * NBLK + b_: e_ * NBLK + b_ + 1], 0)),
                                16)

                    gp.wait_ge(s_xe[0].h, 32 * (NGC - NGC // 2))
                    gp.wait_ge(s_xe[1].h, 32 * (NGC // 2))
                    gathers(0)
                    for e_ in range(E_FFN):
                        if e_ + 1 < E_FFN:
                            gathers(e_ + 1)
                        for b_ in range(NBLK):
                            j = e_ * NBLK + b_
                            if j >= 1:
                                gp.wait_ge(f_sct.h, 16 * j)
                            gp.wait_ge(f_ysc.h, j + 1)
                            f_sct.inc(gp.indirect_dma_start(
                                bass.AP(acc_d, 0, [[D, TC + 1], [1, D]]),
                                IndirectOffsetOnAxis(
                                    idxs32[:, j:j + 1], 0),
                                xgfm[:, (b_ % 2) * D:(b_ % 2) * D + D], None,
                                compute_op=ALU.add), 16)

                @b.tensor
                def _(te):
                    for e_ in range(E_FFN):
                        te.wait_ge(f_gx.h, 64 * (e_ + 1))
                        for b_ in range(NBLK):
                            for dc in range(DC):
                                k = 32 * e_ + b_ * DC + dc
                                if k >= 2:
                                    te.wait_ge(f_te.h, k - 1)
                                f_tr.inc(te.transpose(
                                    pbc2[0:128, (k % 2) * 512:(k % 2) * 512 + 128],
                                    xg[:, b_ * XR + dc * 128: b_ * XR + (dc + 1) * 128],
                                    eye[:]))
                        te.wait_ge(f_xsp.h, e_ + 1)
                        for q_ in range(NQ):
                            idx = e_ * NQ + q_
                            par = idx % 2
                            te.wait_ge(f_wsp.h, 2 * idx + 1)
                            for m in range(NQ):
                                hg = idx * NQ + m
                                if hg >= 2:
                                    te.wait_ge(f_sil.h, hg - 1)
                                nmm = 0
                                for dc in range(DC):
                                    for A, Bv in ((whi, xgh), (whi, xgl), (wlo, xgh)):
                                        inst = te.matmul(
                                            pq[m % 2][:, 0:CAP],
                                            wap(A, par, dc * QF + m * 128, 128),
                                            Bv[:, dc * CAP:(dc + 1) * CAP],
                                            start=(nmm == 0),
                                            stop=(nmm == 3 * DC - 1))
                                        nmm += 1
                                f_h.inc(inst)
                            te.wait_ge(f_wsp.h, 2 * idx + 2)
                            te.wait_ge(f_hsp.h, 4 * (idx + 1))
                            if q_ == 0:
                                te.wait_ge(f_yte.h, 32 * e_)
                            for m2 in range(8):
                                Y = idx * 8 + m2
                                if Y >= 4:
                                    te.wait_ge(f_ye.h, Y - 3)
                                nmm = 0
                                for mc in range(NQ):
                                    for A, Bv in ((whi, hbh), (whi, hbl), (wlo, hbh)):
                                        inst = te.matmul(
                                            pat[:, (m2 % 4) * 512:(m2 % 4) * 512 + CAP],
                                            wap(A, par,
                                                DC * QF + mc * D + m2 * 128, 128),
                                            Bv[:, mc * CAP:(mc + 1) * CAP],
                                            start=(nmm == 0),
                                            stop=(nmm == 3 * NQ - 1))
                                        nmm += 1
                                f_y.inc(inst)
                        te.wait_ge(f_ye.h, 32 * (e_ + 1))
                        for b_ in range(NBLK):
                            for m2 in range(8):
                                k = (e_ * NBLK + b_) * 8 + m2
                                if k >= 4:
                                    te.wait_ge(f_yte.h, k - 3)
                                f_ytr.inc(te.transpose(
                                    pat[0:128, (k % 4) * 512:(k % 4) * 512 + 128],
                                    yfm[:, m2 * CAP + b_ * 128: m2 * CAP + (b_ + 1) * 128],
                                    eye[:]))

                @b.scalar
                def _(s):
                    for e_ in range(E_FFN):
                        for b_ in range(NBLK):
                            for dc in range(DC):
                                k = 32 * e_ + b_ * DC + dc
                                s.wait_ge(f_tr.h, k + 1)
                                if e_ >= 1 and b_ == 0 and dc == 0:
                                    s.wait_ge(f_sct.h, 16 * NBLK * e_)
                                f_te.inc(s.copy(
                                    xgfm[:, dc * CAP + b_ * 128: dc * CAP + (b_ + 1) * 128],
                                    pbc2[0:128, (k % 2) * 512:(k % 2) * 512 + 128]))
                        for q_ in range(NQ):
                            idx = e_ * NQ + q_
                            for m in range(NQ):
                                hg = idx * NQ + m
                                s.wait_ge(f_h.h, hg + 1)
                                if idx >= 1 and m == 0:
                                    s.wait_ge(f_hsp.h, 4 * idx)
                                f_sil.inc(s.activation(
                                    hbuf[:, m * CAP:(m + 1) * CAP],
                                    pq[m % 2][:, 0:CAP], AF.Silu))
                        for b_ in range(NBLK):
                            for m2 in range(8):
                                k = (e_ * NBLK + b_) * 8 + m2
                                j = e_ * NBLK + b_
                                s.wait_ge(f_ytr.h, k + 1)
                                if j >= 2 and m2 == 0:
                                    s.wait_ge(f_sct.h, 16 * j)
                                f_yte.inc(s.copy(
                                    xgfm[:, (b_ % 2) * D + m2 * 128:
                                         (b_ % 2) * D + (m2 + 1) * 128],
                                    pat[0:128, (k % 4) * 512:(k % 4) * 512 + 128]))

                @b.vector
                def _(v):
                    def wchunk_split(idx, hh):
                        j = 2 * idx + hh
                        par = idx % 2
                        v.wait_ge(s_wl[j % 3].h, 16 * (j // 3 + 1))
                        if idx >= 2:
                            if hh == 0:
                                v.wait_ge(f_h.h, 4 * (idx - 1))
                            else:
                                v.wait_ge(f_y.h, 8 * (idx - 1))
                        sl = (j % 3) * HQW
                        dst = par * QW + hh * HQW
                        v.drain()
                        v.tensor_copy(
                            bass.AP(whi, dst, [[2 * QW, 128], [1, HQW]]),
                            wsrc[:, sl:sl + HQW])
                        v.drain()
                        v.tensor_tensor(
                            bass.AP(wlo, dst, [[2 * QW, 128], [1, HQW]]),
                            wsrc[:, sl:sl + HQW],
                            bass.AP(whi, dst, [[2 * QW, 128], [1, HQW]]),
                            ALU.subtract)
                        f_wsp.inc(v.drain())

                    wchunk_split(0, 0)
                    wchunk_split(0, 1)
                    for e_ in range(E_FFN):
                        v.wait_ge(f_gx.h, 64 * (e_ + 1))
                        for b_ in range(NBLK):
                            v.drain()
                            v.tensor_copy(wvcol[:, b_:b_ + 1],
                                          xg[:, b_ * XR + D + e_: b_ * XR + D + e_ + 1])
                            v.drain()
                            f_wc.inc(v.tensor_tensor(
                                wvcol[:, b_:b_ + 1], wvcol[:, b_:b_ + 1],
                                vmf[:, e_ * NBLK + b_:e_ * NBLK + b_ + 1], ALU.mult))
                        v.wait_ge(f_te.h, 32 * (e_ + 1))
                        if e_ >= 1:
                            v.wait_ge(f_h.h, 16 * e_)
                        v.drain()
                        v.tensor_copy(xgh[:], xgfm[:])
                        v.drain()
                        v.tensor_tensor(xgl[:], xgfm[:], xgh[:], ALU.subtract)
                        f_xsp.inc(v.drain())
                        for q_ in range(NQ):
                            idx = e_ * NQ + q_
                            if idx + 1 < E_FFN * NQ:
                                wchunk_split(idx + 1, 0)
                                wchunk_split(idx + 1, 1)
                            for m in range(NQ):
                                hg = idx * NQ + m
                                v.wait_ge(f_sil.h, hg + 1)
                                if idx >= 1 and m == 0:
                                    v.wait_ge(f_y.h, 8 * idx)
                                v.drain()
                                v.tensor_copy(hbh[:, m * CAP:(m + 1) * CAP],
                                              hbuf[:, m * CAP:(m + 1) * CAP])
                                v.drain()
                                f_hsp.inc(v.tensor_tensor(
                                    hbl[:, m * CAP:(m + 1) * CAP],
                                    hbuf[:, m * CAP:(m + 1) * CAP],
                                    hbh[:, m * CAP:(m + 1) * CAP], ALU.subtract))
                            if q_ == 0:
                                v.wait_ge(f_ytr.h, 32 * e_)
                            for m2 in range(8):
                                Y = idx * 8 + m2
                                v.wait_ge(f_y.h, Y + 1)
                                v.drain()
                                if q_ == 0:
                                    f_ye.inc(v.tensor_copy(
                                        yfm[:, m2 * CAP:(m2 + 1) * CAP],
                                        pat[:, (m2 % 4) * 512:(m2 % 4) * 512 + CAP]))
                                else:
                                    f_ye.inc(v.scalar_tensor_tensor(
                                        yfm[:, m2 * CAP:(m2 + 1) * CAP],
                                        pat[:, (m2 % 4) * 512:(m2 % 4) * 512 + CAP], 0.0,
                                        yfm[:, m2 * CAP:(m2 + 1) * CAP],
                                        ALU.bypass, ALU.add))
                        for b_ in range(NBLK):
                            j = e_ * NBLK + b_
                            v.wait_ge(f_yte.h, 8 * (j + 1))
                            v.drain()
                            f_ysc.inc(v.tensor_scalar_mul(
                                xgfm[:, (b_ % 2) * D:(b_ % 2) * D + D],
                                xgfm[:, (b_ % 2) * D:(b_ % 2) * D + D],
                                wvcol[:, b_:b_ + 1]))

        ln_rb = [Sem(stack, nc, "ln_rb0"), Sem(stack, nc, "ln_rb1")]
        ln_dv = Sem(stack, nc, "ln_dv")
        ln_ac = Sem(stack, nc, "ln_ac")
        ln_o = Sem(stack, nc, "ln_o")
        ss = [Sem(stack, nc, "ss0"), Sem(stack, nc, "ss1")]
        with ExitStack() as lst2:
            lsb2 = mksb(lst2)
            atile = [lsb2(f"atile{i}", 128, D) for i in range(2)]
            xtile2 = [lsb2(f"xtile2{i}", 128, D) for i in range(2)]
            sqb = [lsb2(f"sqb{i}", 128, D) for i in range(2)]
            obuf = [lsb2(f"obuf{i}", 128, D) for i in range(2)]
            lmean = lsb2("lmean", 128, 1)
            lvar = lsb2("lvar", 128, 1)
            lm2 = lsb2("lm2", 128, 1)

            with nc.Block() as b:
                @b.sync
                def _(e):
                    e.wait_ge(f_sct.h, 16 * NBLK * E_FFN)
                    for c in range(NGC):
                        if c >= 2:
                            e.wait_ge(ln_o.h, c - 1)
                        ln_rb[c % 2].inc(e.dma_start(
                            atile[c % 2][:],
                            bass.AP(acc_d, c * 128 * D, [[D, 128], [1, D]])), 16)
                        ln_rb[c % 2].inc(e.dma_start(
                            xtile2[c % 2][:],
                            bass.AP(xext_d, c * 128 * XR, [[XR, 128], [1, D]])), 16)

                @b.vector
                def _(v):
                    for c in range(NGC):
                        par = c % 2
                        v.wait_ge(ln_rb[par].h, 32 * (c // 2 + 1))
                        v.drain()
                        v.tensor_tensor(atile[par][:], atile[par][:], xtile2[par][:],
                                        ALU.add)
                        v.drain()
                        ln_dv.inc(v.reduce_sum(lmean[:], atile[par][:], axis=AX.X))
                        v.wait_ge(ln_ac.h, 2 * c + 1)
                        v.drain()
                        v.tensor_scalar_mul(lmean[:], lmean[:], 1.0 / D)
                        v.reduce_sum(lvar[:], sqb[par][:], axis=AX.X)
                        v.drain()
                        v.tensor_scalar_mul(lvar[:], lvar[:], 1.0 / D)
                        v.tensor_tensor(lm2[:], lmean[:], lmean[:], ALU.mult)
                        v.drain()
                        v.tensor_tensor(lvar[:], lvar[:], lm2[:], ALU.subtract)
                        v.drain()
                        ln_dv.inc(v.tensor_scalar_add(lvar[:], lvar[:], 1e-5))
                        v.wait_ge(ln_ac.h, 2 * c + 2)
                        v.reciprocal(lvar[:], lvar[:])
                        v.drain()
                        if c >= 2:
                            v.wait_ge(ss[c % 2].h, 16 * (c // 2))
                        ln_o.inc(v.tensor_scalar(obuf[par][:], atile[par][:],
                                                 lmean[:, 0:1], lvar[:, 0:1],
                                                 ALU.subtract, ALU.mult))

                @b.scalar
                def _(s):
                    for c in range(NGC):
                        par = c % 2
                        s.wait_ge(ln_dv.h, 2 * c + 1)
                        ln_ac.inc(s.activation(sqb[par][:], atile[par][:], AF.Square))
                        s.wait_ge(ln_dv.h, 2 * c + 2)
                        ln_ac.inc(s.activation(lvar[:], lvar[:], AF.Sqrt))
                        s.wait_ge(ln_o.h, c + 1)
                        ss[c % 2].inc(s.dma_start(
                            bass.AP(out_d, c * 128 * D, [[D, 128], [1, D]]),
                            obuf[par][:]), 16)
                    s.wait_ge(ss[0].h, ss[0].n)
                    s.wait_ge(ss[1].h, ss[1].n)

    return nc


def _host_layouts(L0_attn_x, L0_attn_w_qkv, L0_attn_w_out, L0_attn_alpha,
                  L0_attn_beta, L0_ffn_gate_w, L0_ffn_expert_bias, L0_ffn_w1,
                  L0_ffn_w2, TC, HALO, TTOT_):
    x2 = np.asarray(L0_attn_x, dtype=np.float32).reshape(TTOT_, D)
    xT = np.ascontiguousarray(x2.T)
    wqkv = np.ascontiguousarray(np.asarray(L0_attn_w_qkv, dtype=np.float32))
    wout = np.ascontiguousarray(np.asarray(L0_attn_w_out, dtype=np.float32))
    w1 = np.asarray(L0_ffn_w1, dtype=np.float32)
    w2 = np.asarray(L0_ffn_w2, dtype=np.float32)
    w1t = np.ascontiguousarray(w1.transpose(0, 2, 1)).reshape(E_FFN * D, DFF)
    w2t = np.ascontiguousarray(w2.transpose(0, 2, 1)).reshape(E_FFN * DFF, D)
    gwt = np.ascontiguousarray(np.asarray(L0_ffn_gate_w, dtype=np.float32).T)
    ebias = np.ascontiguousarray(
        np.broadcast_to(np.asarray(L0_ffn_expert_bias, dtype=np.float32), (128, E_TOT)))
    alpha = np.asarray(L0_attn_alpha, dtype=np.float32)
    beta = np.asarray(L0_attn_beta, dtype=np.float32)
    alpha_e = np.ascontiguousarray(np.repeat(alpha, HD).reshape(DC, 128).T)
    beta_e = np.ascontiguousarray(np.repeat(beta, HD).reshape(DC, 128).T)
    eyec = np.eye(128, dtype=np.float32)
    onesrow = np.ones((1, 128), dtype=np.float32)
    onescol = np.ones((128, 1), dtype=np.float32)
    bsel8 = np.zeros((4, 2 * 128), dtype=np.float32)
    for g in range(2):
        for p in range(128):
            bsel8[2 * g + p // 64, g * 128 + p] = 1.0
    dselc = np.zeros((128, 8), dtype=np.float32)
    for p in range(128):
        dselc[p, p // 64] = 1.0
        dselc[p, 4 + 2 + p // 64] = 1.0
    ltic = np.zeros((128, 128), dtype=np.float16)
    for k in range(128):
        ltic[k, k + 1:] = 1
    NGC_ = TC // 128
    tokinc = np.zeros((128, 2 * NGC_), dtype=np.float16)
    for c in range(NGC_):
        tokinc[:, 2 * c] = np.arange(128 * c, 128 * (c + 1))
        tokinc[:, 2 * c + 1] = 1
    iotacc = np.broadcast_to(np.arange(512, dtype=np.float16)[None, :], (128, 512)).copy()
    shared = dict(wqkv=wqkv, wout=wout, w1t=w1t, w2t=w2t, gwt=gwt, ebias=ebias,
                  alphae=alpha_e, betae=beta_e, eye=eyec, onesrow=onesrow,
                  onescol=onescol, bsel8=bsel8, dsel=dselc, lti=ltic,
                  tokin=tokinc, iotac=iotacc)
    in_maps = []
    ncores = TTOT_ // TC
    for c in range(ncores):
        xm = np.ascontiguousarray(xT[:, c * TC:(c + 1) * TC])
        if c == 0:
            xh = np.zeros((D, HALO), dtype=np.float32)
            hm = np.zeros((128, 1), dtype=np.float32)
        else:
            xh = np.ascontiguousarray(xT[:, c * TC - HALO: c * TC])
            hm = np.ones((128, 1), dtype=np.float32)
        m = dict(shared)
        m.update(xm=xm, xh=xh, hmask=hm)
        in_maps.append(m)
    return in_maps


def kernel(L0_attn_x, L0_attn_w_qkv, L0_attn_w_out, L0_attn_alpha, L0_attn_beta,
           L0_ffn_gate_w, L0_ffn_expert_bias, L0_ffn_w1, L0_ffn_w2):
    TC = TTOT // NCORES
    HALO = TC // 2
    in_maps = _host_layouts(L0_attn_x, L0_attn_w_qkv, L0_attn_w_out, L0_attn_alpha,
                            L0_attn_beta, L0_ffn_gate_w, L0_ffn_expert_bias,
                            L0_ffn_w1, L0_ffn_w2, TC, HALO, TTOT)
    nc = build_program(TC, HALO)
    res = run_bass_kernel_spmd(nc, in_maps, core_ids=list(range(NCORES)))
    out = np.concatenate([r["out"] for r in res.results], axis=0)
    return np.ascontiguousarray(out).reshape(1, TTOT, D).astype(np.float32)


# version 24
# speedup vs baseline: 1.0445x; 1.0445x over previous
import numpy as np
from contextlib import ExitStack

import concourse.bass as bass
import concourse.mybir as mybir
from concourse.bass_utils import run_bass_kernel_spmd
from concourse.bass import IndirectOffsetOnAxis

F32 = mybir.dt.float32
AF = mybir.ActivationFunctionType
ALU = mybir.AluOpType
AX = mybir.AxisListType

NCORES = 8
D = 1024
DC = D // 128
QKV = 3 * D
HD = 64
TTOT = 16384

ATTN_TILE = 256
HALO_TILE = 256
LN_TILE = 512
FFN_TILE = 512
E_FFN = 8
E_TOT = 12
DFF = 2048
NQ = 4
QF = DFF // NQ

NEG_BIG = -1.0e30
NSWEEP = 8


class Sem:
    def __init__(self, stack, nc, name):
        self.h = stack.enter_context(nc.semaphore(name))
        self.n = 0

    def inc(self, inst, k=1):
        inst.then_inc(self.h, k)
        self.n += k
        return (self, self.n)


def build_program(TC, HALO):
    nc = bass.Bass("TRN2", target_bir_lowering=False)

    xm_d = nc.dram_tensor("xm", [D, TC], F32, kind="ExternalInput")
    xh_d = nc.dram_tensor("xh", [D, HALO], F32, kind="ExternalInput")
    wqkv_d = nc.dram_tensor("wqkv", [D, QKV], F32, kind="ExternalInput")
    wout_d = nc.dram_tensor("wout", [2 * D, D], F32, kind="ExternalInput")
    w1t_d = nc.dram_tensor("w1t", [E_FFN * D, DFF], F32, kind="ExternalInput")
    w2t_d = nc.dram_tensor("w2t", [E_FFN * DFF, D], F32, kind="ExternalInput")
    gwt_d = nc.dram_tensor("gwt", [D, E_TOT], F32, kind="ExternalInput")
    ebias_d = nc.dram_tensor("ebias", [128, E_TOT], F32, kind="ExternalInput")
    alpha_d = nc.dram_tensor("alphae", [128, DC], F32, kind="ExternalInput")
    beta_d = nc.dram_tensor("betae", [128, DC], F32, kind="ExternalInput")
    hmask_d = nc.dram_tensor("hmask", [128, 1], F32, kind="ExternalInput")
    eye_d = nc.dram_tensor("eye", [128, 128], F32, kind="ExternalInput")
    onesrow_d = nc.dram_tensor("onesrow", [1, 128], F32, kind="ExternalInput")
    onescol_d = nc.dram_tensor("onescol", [128, 1], F32, kind="ExternalInput")
    bsel8_d = nc.dram_tensor("bsel8", [4, 2 * 128], F32, kind="ExternalInput")
    dsel_d = nc.dram_tensor("dsel", [128, 8], F32, kind="ExternalInput")
    lti_d = nc.dram_tensor("lti", [128, 128], mybir.dt.float16, kind="ExternalInput")
    tokin_d = nc.dram_tensor("tokin", [128, 2 * (TC // 128)], mybir.dt.float16, kind="ExternalInput")
    iotac_d = nc.dram_tensor("iotac", [128, 512], mybir.dt.float16, kind="ExternalInput")
    out_d = nc.dram_tensor("out", [TC, D], F32, kind="ExternalOutput")

    stack = ExitStack()

    def mksb(st):
        def sb(name, p, f):
            return st.enter_context(nc.sbuf_tensor(name, [p, f], F32))
        return sb

    sb = mksb(stack)

    def psum(name, p, f):
        return stack.enter_context(nc.psum_tensor(name, [p, f], F32))

    with stack:
        dvf = sb("dvf", 128, DC)
        dvs = sb("dvs", 128, DC)
        hmask = sb("hmaskt", 128, 1)
        eye = sb("eyet", 128, 128)
        onesrow = sb("onesrowt", 1, 128)
        onescol = sb("onescolt", 128, 1)
        bsel8 = sb("bsel8t", 4, 2 * 128)
        dsel = sb("dselt", 128, 8)
        lti = stack.enter_context(nc.sbuf_tensor("ltit", [128, 128], mybir.dt.float16))
        tokin = stack.enter_context(nc.sbuf_tensor("tokint", [128, 2 * (TC // 128)], mybir.dt.float16))
        iotac = stack.enter_context(nc.sbuf_tensor("iotact", [128, 512], mybir.dt.float16))
        ebias = sb("ebiast", 128, E_TOT)
        gwt = sb("gwtt", 128, DC * E_TOT)
        st4 = sb("st4", 128, 4 * DC)
        idxi32 = stack.enter_context(nc.sbuf_tensor("idxi32", [128, 32], mybir.dt.int32))
        idxs32 = stack.enter_context(nc.sbuf_tensor("idxs32", [128, 32], mybir.dt.int32))
        idxtmp = sb("idxtmp", 128, 1)
        vmf = sb("vmf", 128, 32)

        mst = ExitStack()
        msb = mksb(mst)

        pq0 = psum("pq0", 128, 512)
        pq1 = psum("pq1", 128, 512)
        pbc2 = psum("pbc2", 128, 1024)
        pat = psum("pat", 128, 4 * 512)
        pq = [pq0, pq1]

        s_ld = Sem(stack, nc, "s_ld")
        s_st = Sem(stack, nc, "s_st")

        def dbcast(vec, col, n):
            return bass.AP(vec, col, [[vec.shape[1], 128], [0, n]])

        s0_ac = Sem(stack, nc, "s0_ac")
        with nc.Block() as b:
            @b.sync
            def _(e):
                for dst, src in [(eye, eye_d), (onesrow, onesrow_d), (onescol, onescol_d),
                                 (bsel8, bsel8_d), (dsel, dsel_d), (lti, lti_d),
                                 (tokin, tokin_d), (iotac, iotac_d),
                                 (ebias, ebias_d), (hmask, hmask_d),
                                 (dvf, alpha_d), (dvs, beta_d)]:
                    s_ld.inc(e.dma_start(dst[:], src[:]), 16)
                s_ld.inc(e.dma_start(
                    bass.AP(gwt, 0, [[DC * E_TOT, 128], [E_TOT, DC], [1, E_TOT]]),
                    bass.AP(gwt_d, 0, [[E_TOT, 128], [128 * E_TOT, DC], [1, E_TOT]])), 16)
            LD0 = s_ld.n

            @b.scalar
            def _(s):
                s.wait_ge(s_ld.h, LD0)
                s.activation(dvf[:], dvf[:], AF.Sigmoid)
                s0_ac.inc(s.activation(dvs[:], dvs[:], AF.Sigmoid))

            @b.vector
            def _(v):
                v.wait_ge(s0_ac.h, 1)
                v.tensor_scalar_max(dvf[:], dvf[:], 1e-8)
                v.drain()
                v.tensor_scalar_max(dvs[:], dvs[:], 1e-8)
                v.drain()

        acc = msb("acc", 128, DC * TC)

        AT = ATTN_TILE
        NAT = TC // AT
        NHH = HALO // AT
        NT = NHH + NAT
        assert NHH % 2 == 0 and NT % 2 == 0
        a_qkv = Sem(stack, nc, "a_qkv")
        a_den = Sem(stack, nc, "a_den")
        a_bc = Sem(stack, nc, "a_bc")
        a_out = Sem(stack, nc, "a_out")
        a_mn = Sem(stack, nc, "a_mn")
        a_phi = Sem(stack, nc, "a_phi")
        a_kv = Sem(stack, nc, "a_kv")
        a_tmp = Sem(stack, nc, "a_tmp")
        a_dden = Sem(stack, nc, "a_dden")
        a_fs = Sem(stack, nc, "a_fs")
        a_acd = Sem(stack, nc, "a_acd")
        a_exp = Sem(stack, nc, "a_exp")
        a_vcp = Sem(stack, nc, "a_vcp")
        a_aca = Sem(stack, nc, "a_aca")

        with ExitStack() as ast:
            asb = mksb(ast)
            BF16 = mybir.dt.bfloat16

            def asbt(name, f):
                return ast.enter_context(nc.sbuf_tensor(name, [128, f], BF16))

            xhi_r = asbt("xhi_r", DC * TC)
            xlo_r = asbt("xlo_r", DC * TC)
            wland = asb("wland", 128, 1024)
            wqh, wql = asbt("wqh", DC * 128), asbt("wql", DC * 128)
            wkh, wkl = asbt("wkh", 2 * DC * 128), asbt("wkl", 2 * DC * 128)
            wvh, wvl = asbt("wvh", 2 * DC * 128), asbt("wvl", 2 * DC * 128)
            woh, wol = asbt("woh", 2 * D), asbt("wol", 2 * D)
            xh_t = asb("xh_t", 128, 2 * 1024)
            xhh = asbt("xhh", 2 * DC * AT)
            xhl = asbt("xhl", 2 * DC * AT)
            qphi = [asb(f"qphi{i}", 128, AT) for i in range(2)]
            runb = [asb(f"runb{i}", 128, 2 * AT) for i in range(2)]
            ksb = [asb(f"ksb{i}", 128, 2 * AT) for i in range(2)]
            tmp1 = asb("tmp1", 128, 2 * AT)
            fsb = asb("fsb", 128, 2 * AT)
            fsh = [asbt(f"fsh{i}", 2 * AT) for i in range(2)]
            fsl = [asbt(f"fsl{i}", 2 * AT) for i in range(2)]
            dengall = asb("dengall", 4, AT)

            tiles = [(sw, i, i < NHH) for sw in range(NSWEEP) for i in range(NT)]
            NG = len(tiles)
            AQa, MNa, EXPa, PHIa, VCPa, TMPa = [], [], [], [], [], []
            DENa, DDENa, BCa, FSa, OUTa, ACCa = [], [], [], [], [], []
            prev_main = []
            cnt = dict(aq=0, mn=0, ex=0, ph=0, vc=0, tp=0, dn=0, dd=0, bc=0,
                       fs=0, ot=0, aca=0, acd=0)
            pm_trk = None
            for g, (sw, i, hal) in enumerate(tiles):
                prev_main.append(pm_trk)
                if hal:
                    cnt["aq"] += 2
                    cnt["mn"] += 1
                    cnt["ex"] += 1
                    cnt["ph"] += 1
                    cnt["vc"] += 1
                    cnt["tp"] += 1
                else:
                    cnt["aq"] += 3
                    cnt["mn"] += 2
                    cnt["ex"] += 2
                    cnt["ph"] += 2
                    cnt["vc"] += 1
                    cnt["tp"] += 1
                    cnt["dn"] += 1
                    cnt["dd"] += 1
                    cnt["bc"] += 2
                    cnt["fs"] += 1
                    cnt["ot"] += 8
                    cnt["aca" if sw == 0 else "acd"] += 1
                    pm_trk = g
                AQa.append(cnt["aq"])
                MNa.append(cnt["mn"])
                EXPa.append(cnt["ex"])
                PHIa.append(cnt["ph"])
                VCPa.append(cnt["vc"])
                TMPa.append(cnt["tp"])
                DENa.append(cnt["dn"])
                DDENa.append(cnt["dd"])
                BCa.append(cnt["bc"])
                FSa.append(cnt["fs"])
                OUTa.append(cnt["ot"])
                ACCa.append(cnt["aca"] if sw == 0 else cnt["acd"])
            ACA_TOTAL = cnt["aca"]

            def accsem(g):
                return a_aca if tiles[g][0] == 0 else a_acd

            s_xi = [Sem(stack, nc, f"s_xi{p}") for p in range(2)]
            f_xis = Sem(stack, nc, "f_xis")
            s_wld = Sem(stack, nc, "s_wld")
            f_wgp = Sem(stack, nc, "f_wgp")
            s_xhd = [Sem(stack, nc, f"s_xhd{p}") for p in range(2)]
            f_xhs = Sem(stack, nc, "f_xhs")

            XCH = min(1024, TC)
            NXC = DC * (TC // XCH)
            with nc.Block() as b:
                @b.sync
                def _(e):
                    for ci in range(NXC):
                        dc = ci // (TC // XCH)
                        off = (ci % (TC // XCH)) * XCH
                        if ci >= 2:
                            e.wait_ge(f_xis.h, ci - 1)
                        s_xi[ci % 2].inc(e.dma_start(
                            bass.AP(xh_t, (ci % 2) * 1024, [[2048, 128], [1, XCH]]),
                            bass.AP(xm_d, dc * 128 * TC + off, [[TC, 128], [1, XCH]])),
                            16)

                @b.vector
                def _(v):
                    for ci in range(NXC):
                        dc = ci // (TC // XCH)
                        off = (ci % (TC // XCH)) * XCH
                        v.wait_ge(s_xi[ci % 2].h, 16 * (ci // 2 + 1))
                        sl = (ci % 2) * 1024
                        v.drain()
                        v.tensor_copy(xhi_r[:, dc * TC + off: dc * TC + off + XCH],
                                      xh_t[:, sl:sl + XCH])
                        v.drain()
                        v.tensor_tensor(xlo_r[:, dc * TC + off: dc * TC + off + XCH],
                                        xh_t[:, sl:sl + XCH],
                                        xhi_r[:, dc * TC + off: dc * TC + off + XCH],
                                        ALU.subtract)
                        f_xis.inc(v.drain())

            WGK = [5 * sw + 2 for sw in range(NSWEEP)]
            WGQ = [5 * sw + 3 for sw in range(NSWEEP)]
            WGO = [5 * sw + 5 for sw in range(NSWEEP)]

            with nc.Block() as b:
                @b.sync
                def _(e):
                    xch = [0]

                    def load_xh_half(sw, i, h):
                        ch = xch[0]
                        xch[0] += 1
                        if ch >= 2:
                            e.wait_ge(f_xhs.h, ch - 1)
                        s_xhd[ch % 2].inc(e.dma_start(
                            bass.AP(xh_t, (ch % 2) * 1024,
                                    [[2048, 128], [256, 4], [1, AT]]),
                            bass.AP(xh_d, 4 * h * 128 * HALO + i * AT,
                                    [[HALO, 128], [128 * HALO, 4], [1, AT]])), 16)

                    wc = [0]
                    for sw in range(NSWEEP):
                        for i in range(min(2, NHH)):
                            load_xh_half(sw, i, 0)
                            load_xh_half(sw, i, 1)
                        for off in (D + 128 * sw, 2 * D + 128 * sw, 128 * sw):
                            if wc[0] >= 1:
                                e.wait_ge(f_wgp.h, wc[0])
                            s_wld.inc(e.dma_start(
                                bass.AP(wland, 0, [[1024, 128], [128, DC], [1, 128]]),
                                bass.AP(wqkv_d, off,
                                        [[QKV, 128], [128 * QKV, DC], [1, 128]])), 16)
                            wc[0] += 1
                        for i in range(2, NHH):
                            load_xh_half(sw, i, 0)
                            load_xh_half(sw, i, 1)
                        for kc in range(2):
                            if wc[0] >= 1:
                                e.wait_ge(f_wgp.h, wc[0])
                            s_wld.inc(e.dma_start(
                                bass.AP(wland, 0, [[1024, 128], [1, 1024]]),
                                bass.AP(wout_d, (kc * D + 128 * sw) * D,
                                        [[D, 128], [1, 1024]])), 16)
                            wc[0] += 1

                @b.gpsimd
                def _(gp):
                    xch = [0]
                    wc = [0]

                    def split_xh(sw, i, h):
                        ch = xch[0]
                        xch[0] += 1
                        hidx = sw * NHH + i
                        g = sw * NT + i
                        par = g % 2
                        gp.wait_ge(s_xhd[ch % 2].h, 16 * (ch // 2 + 1))
                        if hidx >= 2:
                            gpv = (hidx - 2) // NHH * NT + (hidx - 2) % NHH
                            gp.wait_ge(a_qkv.h, AQa[gpv])
                        sl = (ch % 2) * 1024
                        dst = par * DC * AT + h * 1024
                        gp.drain()
                        gp.tensor_copy(xhh[:, dst:dst + 1024], xh_t[:, sl:sl + 1024])
                        gp.drain()
                        gp.tensor_tensor(xhl[:, dst:dst + 1024], xh_t[:, sl:sl + 1024],
                                         xhh[:, dst:dst + 1024], ALU.subtract)
                        f_xhs.inc(gp.drain())

                    def split_w(hi_t, lo_t, off, n, rel, relv):
                        gp.wait_ge(s_wld.h, 16 * (wc[0] + 1))
                        wc[0] += 1
                        if rel is not None:
                            gp.wait_ge(rel.h, relv)
                        gp.drain()
                        gp.tensor_copy(hi_t[:, off:off + n], wland[:, 0:n])
                        gp.drain()
                        gp.tensor_tensor(lo_t[:, off:off + n], wland[:, 0:n],
                                         hi_t[:, off:off + n], ALU.subtract)
                        f_wgp.inc(gp.drain())

                    for sw in range(NSWEEP):
                        po = (sw % 2) * 1024
                        kvrel = AQa[(sw - 1) * NT - 1] if sw >= 2 else None
                        aqrel = AQa[sw * NT - 1] if sw >= 1 else None
                        otrel = OUTa[sw * NT - 1] if sw >= 1 else None
                        for i in range(min(2, NHH)):
                            split_xh(sw, i, 0)
                            split_xh(sw, i, 1)
                        split_w(wkh, wkl, po, 1024,
                                a_qkv if sw >= 2 else None, kvrel)
                        split_w(wvh, wvl, po, 1024, None, None)
                        for i in range(2, NHH):
                            split_xh(sw, i, 0)
                            split_xh(sw, i, 1)
                        split_w(wqh, wql, 0, 1024,
                                a_qkv if sw >= 1 else None, aqrel)
                        for kc in range(2):
                            split_w(woh, wol, kc * D, 1024,
                                    a_out if sw >= 1 and kc == 0 else None, otrel)
                @b.tensor
                def _(te):
                    def den_part(g):
                        sw, i, _ = tiles[g]
                        par = g % 2
                        pm = prev_main[g]
                        if i == NHH:
                            te.wait_ge(f_wgp.h, WGO[sw])
                        te.wait_ge(a_tmp.h, TMPa[g])
                        if pm is not None:
                            te.wait_ge(accsem(pm).h, ACCa[pm])
                        te.matmul(pat[0:4, 0:AT], dsel[:, 0:4],
                                  tmp1[:, 0:AT], start=True, stop=False)
                        a_den.inc(te.matmul(pat[0:4, 0:AT], dsel[:, 4:8],
                                            tmp1[:, AT:2 * AT], start=False,
                                            stop=True))

                    def bc_part(g):
                        par = g % 2
                        pm = prev_main[g]
                        te.wait_ge(a_dden.h, DDENa[g])
                        for gi in range(2):
                            if pm is not None:
                                te.wait_ge(a_fs.h, FSa[pm])
                            a_bc.inc(te.matmul(
                                pbc2[:, gi * 512:gi * 512 + AT],
                                bsel8[:, gi * 128:(gi + 1) * 128],
                                dengall[:, 0:AT], start=True, stop=True))

                    def out_part(g):
                        par = g % 2
                        pm = prev_main[g]
                        te.wait_ge(a_fs.h, FSa[g])
                        if pm is not None:
                            te.wait_ge(accsem(pm).h, ACCa[pm])
                        for m in range(8):
                            nmm = 0
                            for kc in range(2):
                                for A, Bv in ((woh, fsh), (woh, fsl), (wol, fsh)):
                                    inst = te.matmul(
                                        pat[:, m * AT:(m + 1) * AT],
                                        A[:, kc * D + m * 128: kc * D + (m + 1) * 128],
                                        Bv[par][:, kc * AT:(kc + 1) * AT],
                                        start=(nmm == 0), stop=(nmm == 5))
                                    nmm += 1
                            a_out.inc(inst)

                    pend = None
                    for g, (sw, i, hal) in enumerate(tiles):
                        par = g % 2
                        prv = g >= 1 and not tiles[g - 1][2]
                        if i == 0 and pend is not None:
                            out_part(pend)
                            pend = None
                        if hal:
                            hidx = sw * NHH + i
                            if i == 0:
                                te.wait_ge(f_wgp.h, WGK[sw])
                            te.wait_ge(f_xhs.h, 2 * (hidx + 1))
                            po = (sw % 2) * 1024
                            groups = [(1, wkh, wkl, po), (2, wvh, wvl, po)]
                        else:
                            tt = (i - NHH) * AT
                            if i == NHH:
                                te.wait_ge(f_wgp.h, WGQ[sw])
                            po = (sw % 2) * 1024
                            groups = [(0, wqh, wql, 0), (1, wkh, wkl, po),
                                      (2, wvh, wvl, po)]
                        first = True
                        for m, wh_, wl_, wb in groups:
                            if m == 0 and g >= 1:
                                te.wait_ge(a_vcp.h, VCPa[g - 1])
                            if m == 1 and g >= 1:
                                te.wait_ge(a_phi.h, PHIa[g - 1])
                            if m == 2:
                                if hal:
                                    if g >= 1:
                                        te.wait_ge(a_vcp.h, VCPa[g - 1])
                                else:
                                    te.wait_ge(a_phi.h, PHIa[g - 1] + 1)
                            if m == 2 and prv:
                                den_part(g - 1)
                            if hal:
                                xh_ = lambda dc: xhh[:, par * DC * AT + dc * AT:
                                                     par * DC * AT + (dc + 1) * AT]
                                xl_ = lambda dc: xhl[:, par * DC * AT + dc * AT:
                                                     par * DC * AT + (dc + 1) * AT]
                            else:
                                xh_ = lambda dc: xhi_r[:, dc * TC + tt:
                                                       dc * TC + tt + AT]
                                xl_ = lambda dc: xlo_r[:, dc * TC + tt:
                                                       dc * TC + tt + AT]
                            nmm = 0
                            for dc in range(DC):
                                for A, Bv in ((wh_, xh_), (wh_, xl_), (wl_, xh_)):
                                    inst = te.matmul(
                                        pq[[0, 1, 0][m]][:, 0:AT],
                                        A[:, wb + dc * 128: wb + (dc + 1) * 128],
                                        Bv(dc),
                                        start=(nmm == 0), stop=(nmm == 3 * DC - 1))
                                    nmm += 1
                            a_qkv.inc(inst)
                            if first:
                                first = False
                                if pend is not None:
                                    out_part(pend)
                                    pend = None
                        if prv:
                            bc_part(g - 1)
                            pend = g - 1
                    if pend is not None:
                        out_part(pend)
                    den_part(NG - 1)
                    bc_part(NG - 1)
                    out_part(NG - 1)

                @b.vector
                def _(v):
                    def v_tail_den(g):
                        par = g % 2
                        v.wait_ge(a_den.h, DENa[g])
                        v.tensor_scalar_max(dengall[:, 0:AT], pat[0:4, 0:AT],
                                            1e-6)
                        v.drain()
                        v.reciprocal(dengall[:, 0:AT], dengall[:, 0:AT])
                        a_dden.inc(v.drain())

                    def qrep(par):
                        return bass.AP(qphi[par], 0, [[AT, 128], [0, 2], [1, AT]])

                    def v_fs(g):
                        par = g % 2
                        v.wait_ge(a_bc.h, BCa[g])
                        v.drain()
                        v.tensor_tensor(fsb[:, 0:2 * AT], qrep(par),
                                        runb[par][:, 0:2 * AT], ALU.mult)
                        v.drain()
                        v.tensor_tensor(
                            fsb[:, 0:2 * AT], fsb[:, 0:2 * AT],
                            bass.AP(pbc2, 0, [[1024, 128], [512, 2], [1, AT]]),
                            ALU.mult)
                        v.drain()
                        v.tensor_copy(fsh[par][:, 0:2 * AT], fsb[:, 0:2 * AT])
                        v.drain()
                        a_fs.inc(v.tensor_tensor(
                            fsl[par][:, 0:2 * AT], fsb[:, 0:2 * AT],
                            fsh[par][:, 0:2 * AT], ALU.subtract))

                    def v_acc(g):
                        sw, i, _ = tiles[g]
                        tt = (i - NHH) * AT
                        if sw > 0:
                            if sw == 1 and i == NHH:
                                v.wait_ge(a_aca.h, ACA_TOTAL)
                            v.wait_ge(a_out.h, OUTa[g])
                            v.drain()
                            acc_ap = bass.AP(acc, tt,
                                             [[DC * TC, 128], [TC, 8], [1, AT]])
                            a_acd.inc(v.scalar_tensor_tensor(
                                acc_ap,
                                bass.AP(pat, 0, [[4 * 512, 128], [AT, 8], [1, AT]]),
                                0.0, acc_ap, ALU.bypass, ALU.add))

                    for g, (sw, i, hal) in enumerate(tiles):
                        par = g % 2
                        if g >= 2:
                            v.wait_ge(a_out.h, OUTa[g - 2])
                            v.wait_ge(a_bc.h, BCa[g - 2])
                            v.wait_ge(a_den.h, DENa[g - 2])
                            v.wait_ge(a_exp.h, EXPa[g - 2])
                        aqb = AQa[g - 1] if g >= 1 else 0
                        prv = g >= 1 and not tiles[g - 1][2]
                        if hal:
                            v.wait_ge(a_qkv.h, aqb + 1)
                            a_mn.inc(v.tensor_scalar_min(
                                ksb[par][:, AT:2 * AT], pq1[:, 0:AT], 0.0))
                            a_phi.inc(v.tensor_scalar_max(
                                ksb[par][:, 0:AT], pq1[:, 0:AT], 0.0))
                        else:
                            for j in range(2):
                                dst = qphi[par][:, 0:AT] if j == 0 \
                                    else ksb[par][:, AT:2 * AT]
                                sta = runb[par][:, 0:AT] if j == 0 \
                                    else ksb[par][:, 0:AT]
                                v.wait_ge(a_qkv.h, aqb + j + 1)
                                a_mn.inc(v.tensor_scalar_min(
                                    dst, pq[j % 2][:, 0:AT], 0.0))
                                a_phi.inc(v.tensor_scalar_max(
                                    sta, pq[j % 2][:, 0:AT], 0.0))
                        if prv:
                            v_tail_den(g - 1)
                            v_fs(g - 1)
                        if hal:
                            v.wait_ge(a_exp.h, EXPa[g])
                            v.drain()
                            v.tensor_tensor(ksb[par][:, AT:2 * AT],
                                            ksb[par][:, AT:2 * AT],
                                            ksb[par][:, 0:AT], ALU.add)
                            v.drain()
                            v.tensor_scalar_mul(ksb[par][:, AT:2 * AT],
                                                ksb[par][:, AT:2 * AT],
                                                hmask[:, 0:1])
                        else:
                            for j in range(2):
                                dst = qphi[par][:, 0:AT] if j == 0 \
                                    else ksb[par][:, AT:2 * AT]
                                sta = runb[par][:, 0:AT] if j == 0 \
                                    else ksb[par][:, 0:AT]
                                v.wait_ge(a_exp.h, EXPa[g] - 2 + j + 1)
                                v.drain()
                                v.tensor_tensor(dst, dst, sta, ALU.add)
                        v.wait_ge(a_vcp.h, VCPa[g])
                        v.drain()
                        a_kv.inc(v.tensor_tensor(runb[par][:, AT:2 * AT],
                                                 ksb[par][:, AT:2 * AT],
                                                 runb[par][:, AT:2 * AT], ALU.mult))
                        ini = (i == 0)
                        v.drain()
                        v.tensor_tensor_scan(
                            runb[par][:, 0:AT], dbcast(dvf, sw, AT),
                            runb[par][:, AT:2 * AT],
                            0.0 if ini else st4[:, sw:sw + 1],
                            ALU.mult, ALU.add)
                        v.drain()
                        v.tensor_tensor_scan(
                            runb[par][:, AT:2 * AT], dbcast(dvs, sw, AT),
                            runb[par][:, AT:2 * AT],
                            0.0 if ini else st4[:, DC + sw:DC + sw + 1],
                            ALU.mult, ALU.add)
                        v.drain()
                        v.tensor_copy(
                            bass.AP(st4, sw, [[4 * DC, 128], [DC, 2], [1, 1]]),
                            bass.AP(runb[par], AT - 1, [[2 * AT, 128], [AT, 2], [1, 1]]))
                        v.drain()
                        v.tensor_tensor_scan(
                            ksb[par][:, 0:AT], dbcast(dvf, sw, AT),
                            ksb[par][:, AT:2 * AT],
                            0.0 if ini else st4[:, 2 * DC + sw:2 * DC + sw + 1],
                            ALU.mult, ALU.add)
                        v.drain()
                        v.tensor_tensor_scan(
                            ksb[par][:, AT:2 * AT], dbcast(dvs, sw, AT),
                            ksb[par][:, AT:2 * AT],
                            0.0 if ini else st4[:, 3 * DC + sw:3 * DC + sw + 1],
                            ALU.mult, ALU.add)
                        v.drain()
                        v.tensor_copy(
                            bass.AP(st4, 2 * DC + sw, [[4 * DC, 128], [DC, 2], [1, 1]]),
                            bass.AP(ksb[par], AT - 1, [[2 * AT, 128], [AT, 2], [1, 1]]))
                        if not hal:
                            v.drain()
                            v.tensor_tensor(tmp1[:, 0:2 * AT], qrep(par),
                                            ksb[par][:, 0:2 * AT], ALU.mult)
                        a_tmp.inc(v.drain())
                        if prv:
                            v_acc(g - 1)
                    v_tail_den(NG - 1)
                    v_fs(NG - 1)
                    v_acc(NG - 1)

                @b.scalar
                def _(s):
                    def acc_copies(g):
                        tp = (tiles[g][1] - NHH) * AT
                        s.wait_ge(a_out.h, OUTa[g])
                        a_aca.inc(s.copy(
                            bass.AP(acc, tp, [[DC * TC, 128], [TC, 8], [1, AT]]),
                            bass.AP(pat, 0, [[4 * 512, 128], [AT, 8], [1, AT]])))

                    for g, (sw, i, hal) in enumerate(tiles):
                        par = g % 2
                        if hal:
                            s.wait_ge(a_mn.h, MNa[g])
                            a_exp.inc(s.activation(ksb[par][:, AT:2 * AT],
                                                   ksb[par][:, AT:2 * AT], AF.Exp))
                        else:
                            for j in range(2):
                                dst = qphi[par][:, 0:AT] if j == 0 \
                                    else ksb[par][:, AT:2 * AT]
                                s.wait_ge(a_mn.h, MNa[g] - 2 + j + 1)
                                a_exp.inc(s.activation(dst, dst, AF.Exp))
                        s.wait_ge(a_qkv.h, AQa[g])
                        if g >= 2:
                            s.wait_ge(a_fs.h, FSa[g - 2])
                            s.wait_ge(a_tmp.h, TMPa[g - 2])
                        a_vcp.inc(s.copy(runb[par][:, AT:2 * AT], pq0[:, 0:AT]))
                        if g >= 1 and not tiles[g - 1][2] and tiles[g - 1][0] == 0:
                            acc_copies(g - 1)
                    if not tiles[NG - 1][2] and tiles[NG - 1][0] == 0:
                        acc_copies(NG - 1)


        x1 = msb("x1", 128, DC * TC)

        def emit_ln(res_src, tag):
            NLT = TC // LN_TILE
            l_add = Sem(stack, nc, f"l_add{tag}")
            l_s1 = Sem(stack, nc, f"l_s1{tag}")
            l_s2 = Sem(stack, nc, f"l_s2{tag}")
            l_sq = Sem(stack, nc, f"l_sq{tag}")
            l_st1 = Sem(stack, nc, f"l_st1{tag}")
            l_sqr = Sem(stack, nc, f"l_sqr{tag}")
            l_st2 = Sem(stack, nc, f"l_st2{tag}")
            l_bc = Sem(stack, nc, f"l_bc{tag}")
            l_nrm = Sem(stack, nc, f"l_nrm{tag}")
            LT = LN_TILE

            with ExitStack() as lst:
                lsb = mksb(lst)
                lnsum = lsb(f"lnsum{tag}", 1, LT)
                lnsq = lsb(f"lnsq{tag}", 1, LT)
                lnm2 = lsb(f"lnm2{tag}", 1, LT)
                lnsc = lsb(f"lnsc{tag}", 128, 2 * LT)
                xln = lsb(f"xln{tag}", 128, 2 * DC * LT)
                s_lnx = [Sem(stack, nc, f"s_lnx{tag}{p}") for p in range(2)]

                with nc.Block() as b:
                    @b.sync
                    def _(e):
                        for t in range(NLT):
                            if t >= 2:
                                e.wait_ge(l_add.h, t - 1)
                            s_lnx[t % 2].inc(e.dma_start(
                                bass.AP(xln, (t % 2) * DC * LT,
                                        [[2 * DC * LT, 128], [LT, DC], [1, LT]]),
                                bass.AP(xm_d, t * LT,
                                        [[TC, 128], [128 * TC, DC], [1, LT]])), 16)

                    @b.vector
                    def _(v):
                        for t in range(NLT):
                            tt = t * LT
                            v.wait_ge(s_lnx[t % 2].h, 16 * (t // 2 + 1))
                            for dc in range(DC):
                                v.drain()
                                v.tensor_tensor(
                                    x1[:, dc * TC + tt: dc * TC + tt + LT],
                                    xln[:, (t % 2) * DC * LT + dc * LT:
                                        (t % 2) * DC * LT + (dc + 1) * LT],
                                    res_src[:, dc * TC + tt: dc * TC + tt + LT], ALU.add)
                            l_add.inc(v.drain())
                            v.wait_ge(l_s1.h, t + 1)
                            v.wait_ge(l_s2.h, 8 * (t + 1))
                            v.tensor_scalar_mul(lnsum[:, 0:LT], pq0[0:1, 0:LT], 1.0 / D)
                            v.tensor_scalar_mul(lnsq[:, 0:LT], pq1[0:1, 0:LT], 1.0 / D)
                            v.drain()
                            v.tensor_tensor(lnm2[:, 0:LT], lnsum[:, 0:LT], lnsum[:, 0:LT],
                                            ALU.mult)
                            v.drain()
                            v.tensor_tensor(lnsq[:, 0:LT], lnsq[:, 0:LT], lnm2[:, 0:LT],
                                            ALU.subtract)
                            v.drain()
                            l_st1.inc(v.tensor_scalar_add(lnsq[:, 0:LT], lnsq[:, 0:LT],
                                                          1e-5))
                            v.wait_ge(l_sqr.h, t + 1)
                            v.reciprocal(lnsq[:, 0:LT], lnsq[:, 0:LT])
                            v.drain()
                            v.tensor_tensor(lnm2[:, 0:LT], lnsum[:, 0:LT], lnsq[:, 0:LT],
                                            ALU.mult)
                            l_st2.inc(v.drain())
                            v.wait_ge(l_bc.h, 2 * (t + 1))
                            for dc in range(DC):
                                v.drain()
                                v.tensor_tensor(
                                    x1[:, dc * TC + tt: dc * TC + tt + LT],
                                    x1[:, dc * TC + tt: dc * TC + tt + LT],
                                    pbc2[:, 0:LT], ALU.mult)
                                v.drain()
                                inst = v.tensor_tensor(
                                    x1[:, dc * TC + tt: dc * TC + tt + LT],
                                    x1[:, dc * TC + tt: dc * TC + tt + LT],
                                    pbc2[:, 512:512 + LT], ALU.subtract)
                            l_nrm.inc(inst)

                    @b.tensor
                    def _(te):
                        for t in range(NLT):
                            tt = t * LT
                            te.wait_ge(l_add.h, t + 1)
                            if t >= 1:
                                te.wait_ge(l_st1.h, t)
                            for dc in range(DC):
                                inst = te.matmul(pq0[0:1, 0:LT], onescol[:],
                                                 x1[:, dc * TC + tt: dc * TC + tt + LT],
                                                 start=(dc == 0), stop=(dc == DC - 1))
                            l_s1.inc(inst)
                            for dc in range(DC):
                                te.wait_ge(l_sq.h, 8 * t + dc + 1)
                                l_s2.inc(te.matmul(
                                    pq1[0:1, 0:LT], onescol[:],
                                    lnsc[:, (dc % 2) * LT:(dc % 2 + 1) * LT],
                                    start=(dc == 0), stop=(dc == DC - 1)))
                            te.wait_ge(l_st2.h, t + 1)
                            if t >= 1:
                                te.wait_ge(l_nrm.h, t)
                            l_bc.inc(te.matmul(pbc2[:, 0:LT], onesrow[:], lnsq[0:1, 0:LT],
                                               start=True, stop=True))
                            l_bc.inc(te.matmul(pbc2[:, 512:512 + LT], onesrow[:],
                                               lnm2[0:1, 0:LT], start=True, stop=True))

                    @b.scalar
                    def _(s):
                        for t in range(NLT):
                            tt = t * LT
                            s.wait_ge(l_add.h, t + 1)
                            for dc in range(DC):
                                if 8 * t + dc >= 2:
                                    s.wait_ge(l_s2.h, 8 * t + dc - 1)
                                l_sq.inc(s.activation(
                                    lnsc[:, (dc % 2) * LT:(dc % 2 + 1) * LT],
                                    x1[:, dc * TC + tt: dc * TC + tt + LT], AF.Square))
                            s.wait_ge(l_st1.h, t + 1)
                            l_sqr.inc(s.activation(lnsq[:, 0:LT], lnsq[:, 0:LT], AF.Sqrt))

        emit_ln(acc, "A")

        NGC = TC // 128
        CAP = 512
        NBLK = CAP // 128
        XR = D + E_TOT
        xext_d = nc.dram_tensor("xext", [TC, XR], F32, kind="Internal")
        acc_d = nc.dram_tensor("accd", [TC + 1, D], F32, kind="Internal")

        g_pe = Sem(stack, nc, "g_pe")
        g_ev = Sem(stack, nc, "g_ev")
        g_dv1 = Sem(stack, nc, "g_dv1")
        g_ac = Sem(stack, nc, "g_ac")
        g_dv2 = Sem(stack, nc, "g_dv2")
        g_rk = Sem(stack, nc, "g_rk")
        g_chb = Sem(stack, nc, "g_chb")
        g_dvp = Sem(stack, nc, "g_dvp")
        g_chf = Sem(stack, nc, "g_chf")

        r_pd = Sem(stack, nc, "r_pd")
        r_pe = Sem(stack, nc, "r_pe")
        r_ev = Sem(stack, nc, "r_ev")
        r_tr = Sem(stack, nc, "r_tr")
        r_ec = Sem(stack, nc, "r_ec")

        with ExitStack() as gst:
            gsb = mksb(gst)
            glog = gsb("glog", 128, E_TOT)
            gm1 = gsb("gm1", 128, 1)
            gnm1 = gsb("gnm1", 128, 1)
            gm2 = gsb("gm2", 128, 1)
            gex = gsb("gex", 128, E_TOT)
            gtmp = gsb("gtmp", 128, E_TOT)
            gden = gsb("gden", 128, 1)
            gwtok = gsb("gwtok", 128, E_TOT)
            gmall = gsb("gmall", 128, NGC * 8)
            gmi16 = gst.enter_context(nc.sbuf_tensor("gmi16", [128, 8], mybir.dt.float16))
            posall = gsb("posall", 128, NGC * 8)
            gwstash = gsb("gwstash", 128, NGC * E_TOT)
            wztm = gsb("wztm", 128, NGC)
            cntrow = gsb("cntrow", 1, 8)
            chofrow = gsb("chofrow", 1, 8)

            with nc.Block() as b:
                @b.tensor
                def _(te):
                    for c in range(NGC):
                        if c >= 2:
                            te.wait_ge(g_ev.h, c - 1)
                        for dc in range(DC):
                            inst = te.matmul(
                                pq[c % 2][0:128, 0:E_TOT],
                                x1[:, dc * TC + c * 128: dc * TC + (c + 1) * 128],
                                gwt[:, dc * E_TOT:(dc + 1) * E_TOT],
                                start=(dc == 0), stop=(dc == DC - 1))
                        g_pe.inc(inst)
                        te.wait_ge(g_dv2.h, c + 1)
                        if c >= 1:
                            te.wait_ge(g_dvp.h, c)
                            te.wait_ge(g_chf.h, c)
                        te.matmul(pbc2[0:128, 0:8], lti[:], gmi16[:],
                                  start=True, stop=True)
                        g_rk.inc(te.matmul(pbc2[0:1, 512:520], tokin[:, 1:2],
                                           gmi16[:], start=True, stop=True))
                        if c >= 1:
                            te.wait_ge(g_chf.h, c)
                            g_chb.inc(te.matmul(pat[:, 0:8], onesrow[:],
                                                chofrow[0:1, 0:8], start=True, stop=True))

                @b.vector
                def _(v):
                    for c in range(NGC):
                        v.wait_ge(g_pe.h, c + 1)
                        if c >= 1:
                            v.wait_ge(g_ac.h, c)
                        v.drain()
                        g_ev.inc(v.tensor_tensor(glog[:], pq[c % 2][0:128, 0:E_TOT],
                                                 ebias[:], ALU.add))
                        v.drain()
                        v.reduce_max(gm1[:], glog[:], axis=AX.X)
                        v.drain()
                        v.tensor_scalar_mul(gnm1[:], gm1[:], -1.0)
                        v.tensor_scalar(gtmp[:], glog[:], gm1[:, 0:1], None, ALU.is_ge)
                        v.drain()
                        v.scalar_tensor_tensor(gtmp[:], gtmp[:], NEG_BIG, glog[:],
                                               ALU.mult, ALU.add)
                        v.drain()
                        g_dv1.inc(v.reduce_max(gm2[:], gtmp[:], axis=AX.X))
                        v.wait_ge(g_ac.h, c + 1)
                        v.tensor_scalar_add(gden[:], gden[:], 1.0)
                        v.drain()
                        v.reciprocal(gden[:], gden[:])
                        v.tensor_scalar(gwtok[:], glog[:], gm2[:, 0:1], None, ALU.is_ge)
                        v.drain()
                        v.tensor_copy(gmall[:, 8 * c:8 * (c + 1)], gwtok[:, 0:8])
                        v.tensor_scalar(gmi16[:], glog[:, 0:8], gm2[:, 0:1], None,
                                        ALU.is_ge)
                        v.drain()
                        v.tensor_tensor(gwtok[:], gwtok[:], gex[:], ALU.mult)
                        v.drain()
                        g_dv2.inc(v.tensor_scalar_mul(gwtok[:], gwtok[:], gden[:, 0:1]))
                        v.drain()
                        v.tensor_copy(gwstash[:, E_TOT * c:E_TOT * (c + 1)], gwtok[:])
                        v.drain()
                        v.reduce_sum(wztm[:, c:c + 1], gwtok[:, E_FFN:E_TOT], axis=AX.X)
                        v.wait_ge(g_rk.h, c + 1)
                        v.drain()
                        v.tensor_copy(posall[:, 8 * c:8 * (c + 1)], pbc2[0:128, 0:8])
                        if c >= 1:
                            v.wait_ge(g_chb.h, c)
                            v.drain()
                            v.tensor_tensor(posall[:, 8 * c:8 * (c + 1)],
                                            posall[:, 8 * c:8 * (c + 1)],
                                            pat[:, 0:8], ALU.add)
                        g_dvp.inc(v.drain())
                        v.tensor_copy(cntrow[:], pbc2[0:1, 512:520])
                        v.drain()
                        if c == 0:
                            v.tensor_copy(chofrow[:], cntrow[:])
                        else:
                            v.tensor_tensor(chofrow[:], chofrow[:], cntrow[:], ALU.add)
                        g_chf.inc(v.drain())

                @b.scalar
                def _(s):
                    for c in range(NGC):
                        s.wait_ge(g_dv1.h, c + 1)
                        s.activation(gex[:], glog[:], AF.Exp, bias=gnm1[:, 0:1])
                        g_ac.inc(s.activation(gden[:], gm2[:], AF.Exp, bias=gnm1[:, 0:1]))

                @b.gpsimd
                def _(gp):
                    pass


            with ExitStack() as rst:
                rsb = mksb(rst)
                Pb = rst.enter_context(nc.sbuf_tensor("Pb", [128, 2 * CAP], mybir.dt.float16))
                LV = [rsb(f"LV{e}", 2, CAP) for e in range(E_FFN)]

                with nc.Block() as b:
                    @b.vector
                    def _(v):
                        for e_ in range(E_FFN):
                            for c in range(NGC):
                                k = e_ * NGC + c
                                if k >= 2:
                                    v.wait_ge(r_pe.h, k - 1)
                                v.drain()
                                r_pd.inc(v.tensor_scalar(
                                    Pb[:, (c % 2) * CAP:(c % 2 + 1) * CAP],
                                    iotac[:], posall[:, 8 * c + e_:8 * c + e_ + 1],
                                    gmall[:, 8 * c + e_:8 * c + e_ + 1],
                                    ALU.is_equal, ALU.mult))
                            v.wait_ge(r_pe.h, NGC * (e_ + 1))
                            v.drain()
                            r_ev.inc(v.tensor_copy(LV[e_][:], pbc2[0:2, 0:CAP]))
                            for b_ in range(NBLK):
                                j = e_ * NBLK + b_
                                v.wait_ge(r_tr.h, j + 1)
                                v.drain()
                                v.tensor_copy(idxi32[:, j:j + 1],
                                              pat[:, (b_ % 2) * 512:(b_ % 2) * 512 + 1])
                                v.tensor_copy(vmf[:, j:j + 1],
                                              pat[:, (b_ % 2) * 512 + 1:(b_ % 2) * 512 + 2])
                                v.drain()
                                v.scalar_tensor_tensor(
                                    idxtmp[:, 0:1], vmf[:, j:j + 1],
                                    float(-TC), idxi32[:, j:j + 1],
                                    ALU.mult, ALU.add)
                                v.drain()
                                v.tensor_scalar_add(idxtmp[:, 0:1], idxtmp[:, 0:1],
                                                    float(TC))
                                v.drain()
                                r_ec.inc(v.tensor_copy(idxs32[:, j:j + 1],
                                                       idxtmp[:, 0:1]))

                    @b.tensor
                    def _(te):
                        for e_ in range(E_FFN):
                            if e_ >= 1:
                                te.wait_ge(r_ev.h, e_)
                            for c in range(NGC):
                                k = e_ * NGC + c
                                te.wait_ge(r_pd.h, k + 1)
                                r_pe.inc(te.matmul(
                                    pbc2[0:2, 0:CAP], tokin[:, 2 * c:2 * c + 2],
                                    Pb[:, (c % 2) * CAP:(c % 2 + 1) * CAP],
                                    start=(c == 0), stop=(c == NGC - 1)))
                            te.wait_ge(r_ev.h, e_ + 1)
                            for b_ in range(NBLK):
                                j = e_ * NBLK + b_
                                if j >= 2:
                                    te.wait_ge(r_ec.h, j - 1)
                                r_tr.inc(te.transpose(
                                    pat[0:128, (b_ % 2) * 512:(b_ % 2) * 512 + 2],
                                    LV[e_][:, b_ * 128:(b_ + 1) * 128], eye[0:2, 0:2]))

            x_tr = Sem(stack, nc, "x_tr")
            x_ev = Sem(stack, nc, "x_ev")
            x_dv = Sem(stack, nc, "x_dv")
            s_xe = [Sem(stack, nc, "s_xe0"), Sem(stack, nc, "s_xe1")]

            with ExitStack() as xst:
                xsb = mksb(xst)
                xtile = [xsb(f"xtile{i}", 128, XR) for i in range(2)]
                acci = [xsb(f"acci{i}", 128, D) for i in range(2)]

                with nc.Block() as b:
                    @b.tensor
                    def _(te):
                        for c in range(NGC):
                            for dc in range(DC):
                                k = c * DC + dc
                                if k >= 2:
                                    te.wait_ge(x_ev.h, k - 1)
                                x_tr.inc(te.transpose(
                                    pbc2[0:128, (k % 2) * 512:(k % 2) * 512 + 128],
                                    x1[:, dc * TC + c * 128: dc * TC + (c + 1) * 128],
                                    eye[:]))

                    @b.scalar
                    def _(s):
                        for c in range(NGC):
                            for dc in range(DC):
                                k = c * DC + dc
                                s.wait_ge(x_tr.h, k + 1)
                                if c >= 2 and dc == 0:
                                    s.wait_ge(s_xe[c % 2].h, 32 * (c // 2))
                                x_ev.inc(s.copy(
                                    xtile[c % 2][:, dc * 128:(dc + 1) * 128],
                                    pbc2[0:128, (k % 2) * 512:(k % 2) * 512 + 128]))

                    @b.vector
                    def _(v):
                        for c in range(NGC):
                            v.wait_ge(x_ev.h, 8 * (c + 1))
                            if c >= 2:
                                v.wait_ge(s_xe[c % 2].h, 32 * (c // 2))
                            v.drain()
                            v.tensor_copy(xtile[c % 2][:, D:D + E_TOT],
                                          gwstash[:, E_TOT * c:E_TOT * (c + 1)])
                            v.drain()
                            x_dv.inc(v.tensor_scalar_mul(
                                acci[c % 2][:], xtile[c % 2][:, 0:D], wztm[:, c:c + 1]))

                    @b.sync
                    def _(e):
                        for c in range(NGC):
                            e.wait_ge(x_dv.h, c + 1)
                            s_xe[c % 2].inc(e.dma_start(
                                bass.AP(xext_d, c * 128 * XR, [[XR, 128], [1, XR]]),
                                xtile[c % 2][:]), 16)
                            s_xe[c % 2].inc(e.dma_start(
                                bass.AP(acc_d, c * 128 * D, [[D, 128], [1, D]]),
                                acci[c % 2][:]), 16)

        mst.close()

        f_gx = Sem(stack, nc, "f_gx")
        f_sct = Sem(stack, nc, "f_sct")
        f_wc = Sem(stack, nc, "f_wc")
        f_tr = Sem(stack, nc, "f_tr")
        f_te = Sem(stack, nc, "f_te")
        f_h = Sem(stack, nc, "f_h")
        f_sil = Sem(stack, nc, "f_sil")
        f_y = Sem(stack, nc, "f_y")
        f_ye = Sem(stack, nc, "f_ye")
        f_ytr = Sem(stack, nc, "f_ytr")
        f_yte = Sem(stack, nc, "f_yte")
        f_ysc = Sem(stack, nc, "f_ysc")
        f_wsp = Sem(stack, nc, "f_wsp")
        f_xsp = Sem(stack, nc, "f_xsp")
        f_hsp = Sem(stack, nc, "f_hsp")
        s_wl = [Sem(stack, nc, f"s_wl{j}") for j in range(3)]
        QW = DC * QF + NQ * D
        HQW = DC * QF
        assert HQW == NQ * D

        with ExitStack() as fst:
            fsb_ = mksb(fst)
            BF16 = mybir.dt.bfloat16

            def fsbt(name, f):
                return fst.enter_context(nc.sbuf_tensor(name, [128, f], BF16))

            wsrc = fsb_("warena", 128, 3 * HQW)
            whi = fsbt("whi", 2 * QW)
            wlo = fsbt("wlo", 2 * QW)
            xg = fsb_("xg", 128, NBLK * XR)
            xgfm = fsb_("xgfm", 128, DC * CAP)
            xgh = fsbt("xgh", DC * CAP)
            xgl = fsbt("xgl", DC * CAP)
            yfm = fsb_("yfm", 128, 8 * CAP)
            hbuf = fsb_("hbuf", 128, NQ * CAP)
            hbh = fsbt("hbh", NQ * CAP)
            hbl = fsbt("hbl", NQ * CAP)
            wvcol = fsb_("wvcol", 128, NBLK)

            def wap(buf, par, off, m):
                return bass.AP(buf, par * QW + off, [[2 * QW, 128], [1, m]])

            with nc.Block() as b:
                @b.sync
                def _(e):
                    for e_ in range(E_FFN):
                        for q_ in range(NQ):
                            idx = e_ * NQ + q_
                            for hh in range(2):
                                j = 2 * idx + hh
                                if j >= 3:
                                    e.wait_ge(f_wsp.h, j - 2)
                                sl = (j % 3) * HQW
                                if hh == 0:
                                    s_wl[j % 3].inc(e.dma_start(
                                        bass.AP(wsrc, sl,
                                                [[3 * HQW, 128], [QF, DC], [1, QF]]),
                                        bass.AP(w1t_d, e_ * D * DFF + q_ * QF,
                                                [[DFF, 128], [128 * DFF, DC], [1, QF]])),
                                        16)
                                else:
                                    s_wl[j % 3].inc(e.dma_start(
                                        bass.AP(wsrc, sl,
                                                [[3 * HQW, 128], [D, NQ], [1, D]]),
                                        bass.AP(w2t_d, (e_ * DFF + q_ * QF) * D,
                                                [[D, 128], [128 * D, NQ], [1, D]])),
                                        16)

                @b.gpsimd
                def _(gp):
                    def gathers(e_):
                        for b_ in range(NBLK):
                            if e_ >= 1:
                                gp.wait_ge(f_tr.h, 32 * (e_ - 1) + 8 * (b_ + 1))
                                gp.wait_ge(f_wc.h, 4 * (e_ - 1) + b_ + 1)
                            f_gx.inc(gp.indirect_dma_start(
                                xg[:, b_ * XR:(b_ + 1) * XR], None,
                                bass.AP(xext_d, 0, [[XR, TC], [1, XR]]),
                                IndirectOffsetOnAxis(
                                    idxi32[:, e_ * NBLK + b_: e_ * NBLK + b_ + 1], 0)),
                                16)

                    gp.wait_ge(s_xe[0].h, 32 * (NGC - NGC // 2))
                    gp.wait_ge(s_xe[1].h, 32 * (NGC // 2))
                    gathers(0)
                    for e_ in range(E_FFN):
                        if e_ + 1 < E_FFN:
                            gathers(e_ + 1)
                        for b_ in range(NBLK):
                            j = e_ * NBLK + b_
                            if j >= 1:
                                gp.wait_ge(f_sct.h, 16 * j)
                            gp.wait_ge(f_ysc.h, j + 1)
                            f_sct.inc(gp.indirect_dma_start(
                                bass.AP(acc_d, 0, [[D, TC + 1], [1, D]]),
                                IndirectOffsetOnAxis(
                                    idxs32[:, j:j + 1], 0),
                                xgfm[:, (b_ % 2) * D:(b_ % 2) * D + D], None,
                                compute_op=ALU.add), 16)

                @b.tensor
                def _(te):
                    for e_ in range(E_FFN):
                        te.wait_ge(f_gx.h, 64 * (e_ + 1))
                        for b_ in range(NBLK):
                            for dc in range(DC):
                                k = 32 * e_ + b_ * DC + dc
                                if k >= 2:
                                    te.wait_ge(f_te.h, k - 1)
                                f_tr.inc(te.transpose(
                                    pbc2[0:128, (k % 2) * 512:(k % 2) * 512 + 128],
                                    xg[:, b_ * XR + dc * 128: b_ * XR + (dc + 1) * 128],
                                    eye[:]))
                        te.wait_ge(f_xsp.h, e_ + 1)
                        for q_ in range(NQ):
                            idx = e_ * NQ + q_
                            par = idx % 2
                            te.wait_ge(f_wsp.h, 2 * idx + 1)
                            for m in range(NQ):
                                hg = idx * NQ + m
                                if hg >= 2:
                                    te.wait_ge(f_sil.h, hg - 1)
                                nmm = 0
                                for dc in range(DC):
                                    for A, Bv in ((whi, xgh), (whi, xgl), (wlo, xgh)):
                                        inst = te.matmul(
                                            pq[m % 2][:, 0:CAP],
                                            wap(A, par, dc * QF + m * 128, 128),
                                            Bv[:, dc * CAP:(dc + 1) * CAP],
                                            start=(nmm == 0),
                                            stop=(nmm == 3 * DC - 1))
                                        nmm += 1
                                f_h.inc(inst)
                            te.wait_ge(f_wsp.h, 2 * idx + 2)
                            te.wait_ge(f_hsp.h, 4 * (idx + 1))
                            if q_ == 0:
                                te.wait_ge(f_yte.h, 32 * e_)
                            for m2 in range(8):
                                Y = idx * 8 + m2
                                if Y >= 4:
                                    te.wait_ge(f_ye.h, Y - 3)
                                nmm = 0
                                for mc in range(NQ):
                                    for A, Bv in ((whi, hbh), (whi, hbl), (wlo, hbh)):
                                        inst = te.matmul(
                                            pat[:, (m2 % 4) * 512:(m2 % 4) * 512 + CAP],
                                            wap(A, par,
                                                DC * QF + mc * D + m2 * 128, 128),
                                            Bv[:, mc * CAP:(mc + 1) * CAP],
                                            start=(nmm == 0),
                                            stop=(nmm == 3 * NQ - 1))
                                        nmm += 1
                                f_y.inc(inst)
                        te.wait_ge(f_ye.h, 32 * (e_ + 1))
                        for b_ in range(NBLK):
                            for m2 in range(8):
                                k = (e_ * NBLK + b_) * 8 + m2
                                if k >= 4:
                                    te.wait_ge(f_yte.h, k - 3)
                                f_ytr.inc(te.transpose(
                                    pat[0:128, (k % 4) * 512:(k % 4) * 512 + 128],
                                    yfm[:, m2 * CAP + b_ * 128: m2 * CAP + (b_ + 1) * 128],
                                    eye[:]))

                @b.scalar
                def _(s):
                    for e_ in range(E_FFN):
                        for b_ in range(NBLK):
                            for dc in range(DC):
                                k = 32 * e_ + b_ * DC + dc
                                s.wait_ge(f_tr.h, k + 1)
                                if e_ >= 1 and b_ == 0 and dc == 0:
                                    s.wait_ge(f_sct.h, 16 * NBLK * e_)
                                f_te.inc(s.copy(
                                    xgfm[:, dc * CAP + b_ * 128: dc * CAP + (b_ + 1) * 128],
                                    pbc2[0:128, (k % 2) * 512:(k % 2) * 512 + 128]))
                        for q_ in range(NQ):
                            idx = e_ * NQ + q_
                            for m in range(NQ):
                                hg = idx * NQ + m
                                s.wait_ge(f_h.h, hg + 1)
                                if idx >= 1 and m == 0:
                                    s.wait_ge(f_hsp.h, 4 * idx)
                                f_sil.inc(s.activation(
                                    hbuf[:, m * CAP:(m + 1) * CAP],
                                    pq[m % 2][:, 0:CAP], AF.Silu))
                        for b_ in range(NBLK):
                            for m2 in range(8):
                                k = (e_ * NBLK + b_) * 8 + m2
                                j = e_ * NBLK + b_
                                s.wait_ge(f_ytr.h, k + 1)
                                if j >= 2 and m2 == 0:
                                    s.wait_ge(f_sct.h, 16 * j)
                                f_yte.inc(s.copy(
                                    xgfm[:, (b_ % 2) * D + m2 * 128:
                                         (b_ % 2) * D + (m2 + 1) * 128],
                                    pat[0:128, (k % 4) * 512:(k % 4) * 512 + 128]))

                @b.vector
                def _(v):
                    def wchunk_split(idx, hh):
                        j = 2 * idx + hh
                        par = idx % 2
                        v.wait_ge(s_wl[j % 3].h, 16 * (j // 3 + 1))
                        if idx >= 2:
                            if hh == 0:
                                v.wait_ge(f_h.h, 4 * (idx - 1))
                            else:
                                v.wait_ge(f_y.h, 8 * (idx - 1))
                        sl = (j % 3) * HQW
                        dst = par * QW + hh * HQW
                        v.drain()
                        v.tensor_copy(
                            bass.AP(whi, dst, [[2 * QW, 128], [1, HQW]]),
                            wsrc[:, sl:sl + HQW])
                        v.drain()
                        v.tensor_tensor(
                            bass.AP(wlo, dst, [[2 * QW, 128], [1, HQW]]),
                            wsrc[:, sl:sl + HQW],
                            bass.AP(whi, dst, [[2 * QW, 128], [1, HQW]]),
                            ALU.subtract)
                        f_wsp.inc(v.drain())

                    wchunk_split(0, 0)
                    wchunk_split(0, 1)
                    for e_ in range(E_FFN):
                        v.wait_ge(f_gx.h, 64 * (e_ + 1))
                        for b_ in range(NBLK):
                            v.drain()
                            v.tensor_copy(wvcol[:, b_:b_ + 1],
                                          xg[:, b_ * XR + D + e_: b_ * XR + D + e_ + 1])
                            v.drain()
                            f_wc.inc(v.tensor_tensor(
                                wvcol[:, b_:b_ + 1], wvcol[:, b_:b_ + 1],
                                vmf[:, e_ * NBLK + b_:e_ * NBLK + b_ + 1], ALU.mult))
                        v.wait_ge(f_te.h, 32 * (e_ + 1))
                        if e_ >= 1:
                            v.wait_ge(f_h.h, 16 * e_)
                        v.drain()
                        v.tensor_copy(xgh[:], xgfm[:])
                        v.drain()
                        v.tensor_tensor(xgl[:], xgfm[:], xgh[:], ALU.subtract)
                        f_xsp.inc(v.drain())
                        for q_ in range(NQ):
                            idx = e_ * NQ + q_
                            if idx + 1 < E_FFN * NQ:
                                wchunk_split(idx + 1, 0)
                                wchunk_split(idx + 1, 1)
                            for m in range(NQ):
                                hg = idx * NQ + m
                                v.wait_ge(f_sil.h, hg + 1)
                                if idx >= 1 and m == 0:
                                    v.wait_ge(f_y.h, 8 * idx)
                                v.drain()
                                v.tensor_copy(hbh[:, m * CAP:(m + 1) * CAP],
                                              hbuf[:, m * CAP:(m + 1) * CAP])
                                v.drain()
                                f_hsp.inc(v.tensor_tensor(
                                    hbl[:, m * CAP:(m + 1) * CAP],
                                    hbuf[:, m * CAP:(m + 1) * CAP],
                                    hbh[:, m * CAP:(m + 1) * CAP], ALU.subtract))
                            if q_ == 0:
                                v.wait_ge(f_ytr.h, 32 * e_)
                            for m2 in range(8):
                                Y = idx * 8 + m2
                                v.wait_ge(f_y.h, Y + 1)
                                v.drain()
                                if q_ == 0:
                                    f_ye.inc(v.tensor_copy(
                                        yfm[:, m2 * CAP:(m2 + 1) * CAP],
                                        pat[:, (m2 % 4) * 512:(m2 % 4) * 512 + CAP]))
                                else:
                                    f_ye.inc(v.scalar_tensor_tensor(
                                        yfm[:, m2 * CAP:(m2 + 1) * CAP],
                                        pat[:, (m2 % 4) * 512:(m2 % 4) * 512 + CAP], 0.0,
                                        yfm[:, m2 * CAP:(m2 + 1) * CAP],
                                        ALU.bypass, ALU.add))
                        for b_ in range(NBLK):
                            j = e_ * NBLK + b_
                            v.wait_ge(f_yte.h, 8 * (j + 1))
                            v.drain()
                            f_ysc.inc(v.tensor_scalar_mul(
                                xgfm[:, (b_ % 2) * D:(b_ % 2) * D + D],
                                xgfm[:, (b_ % 2) * D:(b_ % 2) * D + D],
                                wvcol[:, b_:b_ + 1]))

        ln_rb = [Sem(stack, nc, "ln_rb0"), Sem(stack, nc, "ln_rb1")]
        ln_dv = Sem(stack, nc, "ln_dv")
        ln_ac = Sem(stack, nc, "ln_ac")
        ln_o = Sem(stack, nc, "ln_o")
        ss = [Sem(stack, nc, "ss0"), Sem(stack, nc, "ss1")]
        with ExitStack() as lst2:
            lsb2 = mksb(lst2)
            atile = [lsb2(f"atile{i}", 128, D) for i in range(2)]
            xtile2 = [lsb2(f"xtile2{i}", 128, D) for i in range(2)]
            sqb = [lsb2(f"sqb{i}", 128, D) for i in range(2)]
            obuf = [lsb2(f"obuf{i}", 128, D) for i in range(2)]
            lmean = lsb2("lmean", 128, 1)
            lvar = lsb2("lvar", 128, 1)
            lm2 = lsb2("lm2", 128, 1)

            with nc.Block() as b:
                @b.sync
                def _(e):
                    e.wait_ge(f_sct.h, 16 * NBLK * E_FFN)
                    for c in range(NGC):
                        if c >= 2:
                            e.wait_ge(ln_o.h, c - 1)
                        ln_rb[c % 2].inc(e.dma_start(
                            atile[c % 2][:],
                            bass.AP(acc_d, c * 128 * D, [[D, 128], [1, D]])), 16)
                        ln_rb[c % 2].inc(e.dma_start(
                            xtile2[c % 2][:],
                            bass.AP(xext_d, c * 128 * XR, [[XR, 128], [1, D]])), 16)

                @b.vector
                def _(v):
                    for c in range(NGC):
                        par = c % 2
                        v.wait_ge(ln_rb[par].h, 32 * (c // 2 + 1))
                        v.drain()
                        v.tensor_tensor(atile[par][:], atile[par][:], xtile2[par][:],
                                        ALU.add)
                        v.drain()
                        ln_dv.inc(v.reduce_sum(lmean[:], atile[par][:], axis=AX.X))
                        v.wait_ge(ln_ac.h, 2 * c + 1)
                        v.drain()
                        v.tensor_scalar_mul(lmean[:], lmean[:], 1.0 / D)
                        v.reduce_sum(lvar[:], sqb[par][:], axis=AX.X)
                        v.drain()
                        v.tensor_scalar_mul(lvar[:], lvar[:], 1.0 / D)
                        v.tensor_tensor(lm2[:], lmean[:], lmean[:], ALU.mult)
                        v.drain()
                        v.tensor_tensor(lvar[:], lvar[:], lm2[:], ALU.subtract)
                        v.drain()
                        ln_dv.inc(v.tensor_scalar_add(lvar[:], lvar[:], 1e-5))
                        v.wait_ge(ln_ac.h, 2 * c + 2)
                        v.reciprocal(lvar[:], lvar[:])
                        v.drain()
                        if c >= 2:
                            v.wait_ge(ss[c % 2].h, 16 * (c // 2))
                        ln_o.inc(v.tensor_scalar(obuf[par][:], atile[par][:],
                                                 lmean[:, 0:1], lvar[:, 0:1],
                                                 ALU.subtract, ALU.mult))

                @b.scalar
                def _(s):
                    for c in range(NGC):
                        par = c % 2
                        s.wait_ge(ln_dv.h, 2 * c + 1)
                        ln_ac.inc(s.activation(sqb[par][:], atile[par][:], AF.Square))
                        s.wait_ge(ln_dv.h, 2 * c + 2)
                        ln_ac.inc(s.activation(lvar[:], lvar[:], AF.Sqrt))
                        s.wait_ge(ln_o.h, c + 1)
                        ss[c % 2].inc(s.dma_start(
                            bass.AP(out_d, c * 128 * D, [[D, 128], [1, D]]),
                            obuf[par][:]), 16)
                    s.wait_ge(ss[0].h, ss[0].n)
                    s.wait_ge(ss[1].h, ss[1].n)

    return nc


def _host_layouts(L0_attn_x, L0_attn_w_qkv, L0_attn_w_out, L0_attn_alpha,
                  L0_attn_beta, L0_ffn_gate_w, L0_ffn_expert_bias, L0_ffn_w1,
                  L0_ffn_w2, TC, HALO, TTOT_):
    x2 = np.asarray(L0_attn_x, dtype=np.float32).reshape(TTOT_, D)
    xT = np.ascontiguousarray(x2.T)
    wqkv = np.ascontiguousarray(np.asarray(L0_attn_w_qkv, dtype=np.float32))
    wout = np.ascontiguousarray(np.asarray(L0_attn_w_out, dtype=np.float32))
    w1 = np.asarray(L0_ffn_w1, dtype=np.float32)
    w2 = np.asarray(L0_ffn_w2, dtype=np.float32)
    w1t = np.ascontiguousarray(w1.transpose(0, 2, 1)).reshape(E_FFN * D, DFF)
    w2t = np.ascontiguousarray(w2.transpose(0, 2, 1)).reshape(E_FFN * DFF, D)
    gwt = np.ascontiguousarray(np.asarray(L0_ffn_gate_w, dtype=np.float32).T)
    ebias = np.ascontiguousarray(
        np.broadcast_to(np.asarray(L0_ffn_expert_bias, dtype=np.float32), (128, E_TOT)))
    alpha = np.asarray(L0_attn_alpha, dtype=np.float32)
    beta = np.asarray(L0_attn_beta, dtype=np.float32)
    alpha_e = np.ascontiguousarray(np.repeat(alpha, HD).reshape(DC, 128).T)
    beta_e = np.ascontiguousarray(np.repeat(beta, HD).reshape(DC, 128).T)
    eyec = np.eye(128, dtype=np.float32)
    onesrow = np.ones((1, 128), dtype=np.float32)
    onescol = np.ones((128, 1), dtype=np.float32)
    bsel8 = np.zeros((4, 2 * 128), dtype=np.float32)
    for g in range(2):
        for p in range(128):
            bsel8[2 * g + p // 64, g * 128 + p] = 1.0
    dselc = np.zeros((128, 8), dtype=np.float32)
    for p in range(128):
        dselc[p, p // 64] = 1.0
        dselc[p, 4 + 2 + p // 64] = 1.0
    ltic = np.zeros((128, 128), dtype=np.float16)
    for k in range(128):
        ltic[k, k + 1:] = 1
    NGC_ = TC // 128
    tokinc = np.zeros((128, 2 * NGC_), dtype=np.float16)
    for c in range(NGC_):
        tokinc[:, 2 * c] = np.arange(128 * c, 128 * (c + 1))
        tokinc[:, 2 * c + 1] = 1
    iotacc = np.broadcast_to(np.arange(512, dtype=np.float16)[None, :], (128, 512)).copy()
    shared = dict(wqkv=wqkv, wout=wout, w1t=w1t, w2t=w2t, gwt=gwt, ebias=ebias,
                  alphae=alpha_e, betae=beta_e, eye=eyec, onesrow=onesrow,
                  onescol=onescol, bsel8=bsel8, dsel=dselc, lti=ltic,
                  tokin=tokinc, iotac=iotacc)
    in_maps = []
    ncores = TTOT_ // TC
    for c in range(ncores):
        xm = np.ascontiguousarray(xT[:, c * TC:(c + 1) * TC])
        if c == 0:
            xh = np.zeros((D, HALO), dtype=np.float32)
            hm = np.zeros((128, 1), dtype=np.float32)
        else:
            xh = np.ascontiguousarray(xT[:, c * TC - HALO: c * TC])
            hm = np.ones((128, 1), dtype=np.float32)
        m = dict(shared)
        m.update(xm=xm, xh=xh, hmask=hm)
        in_maps.append(m)
    return in_maps


def kernel(L0_attn_x, L0_attn_w_qkv, L0_attn_w_out, L0_attn_alpha, L0_attn_beta,
           L0_ffn_gate_w, L0_ffn_expert_bias, L0_ffn_w1, L0_ffn_w2):
    TC = TTOT // NCORES
    HALO = TC // 2
    in_maps = _host_layouts(L0_attn_x, L0_attn_w_qkv, L0_attn_w_out, L0_attn_alpha,
                            L0_attn_beta, L0_ffn_gate_w, L0_ffn_expert_bias,
                            L0_ffn_w1, L0_ffn_w2, TC, HALO, TTOT)
    nc = build_program(TC, HALO)
    res = run_bass_kernel_spmd(nc, in_maps, core_ids=list(range(NCORES)))
    out = np.concatenate([r["out"] for r in res.results], axis=0)
    return np.ascontiguousarray(out).reshape(1, TTOT, D).astype(np.float32)
